# version 08
# baseline (speedup 1.0000x reference)
import numpy as np
from contextlib import ExitStack

import concourse.bass as bass
import concourse.mybir as mybir
from concourse.bass_utils import run_bass_kernel_spmd
from concourse.alu_op_type import AluOpType as OP

F32 = mybir.dt.float32
BF16 = mybir.dt.bfloat16
U32 = mybir.dt.uint32
AF = mybir.ActivationFunctionType
AX = mybir.AxisListType

P = 128
D = 2048
SEQ = 2048
TB = 16
TBM = 8
DC = 16
H = 1024
UB = 8
NE = 8
NEXP = 9
EPS = 1e-5
INV_D = 1.0 / D

NATW = 2
W1WIN = 4
W2WIN = 4


class Sem:
    def __init__(self, h):
        self.h = h
        self.n = 0

    def inc(self, instr, k=1):
        instr.then_inc(self.h, k)
        self.n += k
        return self.n


def flat2(ap):
    return ap.rearrange("p a b -> p (a b)")


def build(want_taps=(), stage="full"):
    nc = bass.Bass("TRN2", target_bir_lowering=False, debug=False)

    x = nc.dram_tensor("x", [SEQ, D], F32, kind="ExternalInput").ap()
    gwT = nc.dram_tensor("gwT", [D, NE], F32, kind="ExternalInput").ap()
    w1 = nc.dram_tensor("w1", [NE, H, D], F32, kind="ExternalInput").ap()
    w2 = nc.dram_tensor("w2", [NE, D, H], F32, kind="ExternalInput").ap()
    w3 = nc.dram_tensor("w3", [NE, H, D], F32, kind="ExternalInput").ap()
    ws1 = nc.dram_tensor("ws1", [H, D], F32, kind="ExternalInput").ap()
    ws2 = nc.dram_tensor("ws2", [D, H], F32, kind="ExternalInput").ap()
    ws3 = nc.dram_tensor("ws3", [H, D], F32, kind="ExternalInput").ap()
    out = nc.dram_tensor("out", [TBM * P, D], F32, kind="ExternalOutput").ap()

    x_t = x.rearrange("(tb p) d -> p tb d", p=P)
    out_t = out.rearrange("(tb p) d -> p tb d", p=P)

    with ExitStack() as top:
        e = top.enter_context
        acc = e(nc.sbuf_tensor("acc", [P, TBM, D], F32))
        identt = e(nc.sbuf_tensor("identt", [P, P], F32))
        selt = e(nc.sbuf_tensor("selt", [NE, NE * P], F32))
        iota8 = e(nc.sbuf_tensor("iota8", [P, NE], F32))
        gT = e(nc.sbuf_tensor("gT", [P, DC, NE], BF16))
        gws = e(nc.sbuf_tensor("gws", [P, DC, NE], F32))
        st_sum = e(nc.sbuf_tensor("st_sum", [P, 1], F32))
        st_ssq = e(nc.sbuf_tensor("st_ssq", [P, 1], F32))
        st_mean = e(nc.sbuf_tensor("st_mean", [P, 1], F32))
        st_var = e(nc.sbuf_tensor("st_var", [P, 1], F32))
        st_std = e(nc.sbuf_tensor("st_std", [P, 1], F32))
        st_rstd = e(nc.sbuf_tensor("st_rstd", [P, 1], F32))
        st_nmr = e(nc.sbuf_tensor("st_nmr", [P, 1], F32))
        st_tmp = e(nc.sbuf_tensor("st_tmp", [P, 1], F32))
        sqscr = e(nc.sbuf_tensor("sqscr", [P, D], F32))
        sm_mxn = e(nc.sbuf_tensor("sm_mxn", [P, 2], F32))
        sm_rs = e(nc.sbuf_tensor("sm_rs", [P, 2], F32))
        sm_lnr = e(nc.sbuf_tensor("sm_lnr", [P, 2], F32))
        sm_b2 = e(nc.sbuf_tensor("sm_b2", [P, 2], F32))

        pA = e(nc.psum_tensor("pA", [P, 512 * 4], F32))
        pB = e(nc.psum_tensor("pB", [P, 512 * 4], F32))
        pAR = [pA, pB]

        LDa = Sem(e(nc.semaphore("LDa")))
        LDx = Sem(e(nc.semaphore("LDx")))
        LDg = Sem(e(nc.semaphore("LDg")))
        LDn = [Sem(e(nc.semaphore(f"LDn{i}"))) for i in range(NATW)]
        LDo = Sem(e(nc.semaphore("LDo")))
        GP = Sem(e(nc.semaphore("GP")))
        TP = Sem(e(nc.semaphore("TP")))
        PE = Sem(e(nc.semaphore("PE")))
        VE = Sem(e(nc.semaphore("VE")))
        SE = Sem(e(nc.semaphore("SE")))

        block = e(nc.Block())

        def tap(name, ap, waits=()):
            if name not in want_taps:
                return
            t = nc.dram_tensor("tap_" + name, list(ap.shape), ap.dtype,
                               kind="ExternalOutput").ap()

            @block.gpsimd
            def _(g, ap=ap, t=t, waits=tuple(waits)):
                for s, v in waits:
                    g.wait_ge(s.h, v)
                LDo.inc(g.dma_start(out=t, in_=ap), 16)

        @block.gpsimd
        def _(g):
            GP.inc(g.iota(iota8[:], pattern=[[1, NE]], base=0, channel_multiplier=0,
                          allow_small_or_imprecise_dtypes=True), 1)
            GP.inc(g.iota(identt[:], pattern=[[0, P]], base=0, channel_multiplier=1,
                          allow_small_or_imprecise_dtypes=True), 1)
            GP.inc(g.iota(sqscr[:, 0:P], pattern=[[1, P]], base=0, channel_multiplier=0,
                          allow_small_or_imprecise_dtypes=True), 1)
            GP.inc(g.iota(selt[:], pattern=[[0, NE * P]], base=0, channel_multiplier=1,
                          allow_small_or_imprecise_dtypes=True), 1)
            GP.inc(g.iota(sqscr[0:NE, P:P + NE * P].rearrange("k (e j) -> k e j", j=P),
                          pattern=[[1, NE], [0, P]], base=0, channel_multiplier=0,
                          allow_small_or_imprecise_dtypes=True), 1)
            LDg.inc(g.dma_start(out=gws[:], in_=gwT.rearrange("(m p) e -> p m e", p=P)), 16)
            LDa.inc(g.dma_start(out=acc[:], in_=x_t[:, 0:TBM, :]), 16)
        gp_consts = GP.n
        ld_gws = (LDg, LDg.n)
        ld_acc = (LDa, LDa.n)

        @block.vector
        def _(v):
            v.wait_ge(GP.h, gp_consts)
            v.wait_ge(ld_gws[0].h, ld_gws[1])
            VE.inc(v.tensor_tensor(identt[:], identt[:], sqscr[:, 0:P], OP.is_equal))
            VE.inc(v.tensor_tensor(selt[:], selt[:], sqscr[0:NE, P:P + NE * P], OP.is_equal))
            VE.inc(v.tensor_copy(gT[:], gws[:]))
        ve_consts = VE.n

        st_m2 = e(nc.sbuf_tensor("st_m2", [P, 1], F32))

        def emit_ln(src_ap, dst_ap, ld_wait=None, scalar_pre=None, vector_pre=None):
            @block.vector
            def _(v):
                if ld_wait is not None:
                    v.wait_ge(ld_wait[0].h, ld_wait[1])
                if vector_pre is not None:
                    v.wait_ge(vector_pre[0].h, vector_pre[1])
                VE.inc(v.tensor_reduce(st_sum[:], src_ap, AX.X, OP.add))
            v_red = VE.n

            @block.scalar
            def _(s):
                if ld_wait is not None:
                    s.wait_ge(ld_wait[0].h, ld_wait[1])
                if scalar_pre is not None:
                    s.wait_ge(scalar_pre[0].h, scalar_pre[1])
                SE.inc(s.activation(out=sqscr[:], in_=src_ap, func=AF.Square,
                                    accum_out=st_ssq[:]))
            s_sq = SE.n

            @block.vector
            def _(v):
                v.wait_ge(VE.h, v_red)
                v.wait_ge(SE.h, s_sq)
                VE.inc(v.tensor_scalar_mul(st_mean[:], st_sum[:], INV_D))
                VE.inc(v.tensor_scalar_mul(st_tmp[:], st_ssq[:], INV_D))
            v_d1 = VE.n

            @block.vector
            def _(v):
                v.wait_ge(VE.h, v_d1)
                VE.inc(v.tensor_tensor(st_m2[:], st_mean[:], st_mean[:], OP.mult))
            v_d2 = VE.n

            @block.vector
            def _(v):
                v.wait_ge(VE.h, v_d2)
                VE.inc(v.tensor_tensor(st_var[:], st_tmp[:], st_m2[:], OP.subtract))
            v_d3 = VE.n

            @block.vector
            def _(v):
                v.wait_ge(VE.h, v_d3)
                VE.inc(v.tensor_scalar_add(st_var[:], st_var[:], EPS))
            v_d4 = VE.n

            @block.scalar
            def _(s):
                s.wait_ge(VE.h, v_d4)
                SE.inc(s.activation(out=st_std[:], in_=st_var[:], func=AF.Sqrt))
            s_std = SE.n

            @block.vector
            def _(v):
                v.wait_ge(SE.h, s_std)
                VE.inc(v.reciprocal(st_rstd[:], st_std[:]))
            v_rec = VE.n

            @block.vector
            def _(v):
                v.wait_ge(VE.h, v_rec)
                VE.inc(v.tensor_tensor(st_nmr[:], st_mean[:], st_rstd[:], OP.mult))
            v_d5 = VE.n

            @block.vector
            def _(v):
                v.wait_ge(VE.h, v_d5)
                VE.inc(v.tensor_scalar_mul(st_nmr[:], st_nmr[:], -1.0))
            v_nmr = VE.n

            @block.scalar
            def _(s):
                s.wait_ge(VE.h, v_nmr)
                SE.inc(s.activation(out=dst_ap, in_=src_ap, func=AF.Identity,
                                    bias=st_nmr[:], scale=st_rstd[:]))
            return SE.n

        arena_free = [(None, 0), (None, 0)]
        with ExitStack() as attn_scope:
            ea = attn_scope.enter_context
            Xn = ea(nc.sbuf_tensor("Xn", [P, TB, D], BF16))
            XnT = ea(nc.sbuf_tensor("XnT", [P, DC, TBM, P], BF16))
            xring = [ea(nc.sbuf_tensor("xring0", [P, D], F32))]
            Asb = [ea(nc.sbuf_tensor("Asb0", [P, D], BF16))]
            ATsl = [ea(nc.sbuf_tensor("ATsl0", [P, DC, 4, P], BF16))]

            s_norm = []
            ring_free = [(None, 0), (None, 0)]
            for tb in range(TB):
                if tb < TBM:
                    src = acc[:, tb, :]
                    ldw = ld_acc
                else:
                    slot = 0

                    @block.gpsimd
                    def _(g, slot=slot, tb=tb, gate=tuple(ring_free[slot])):
                        sm, val = gate
                        if sm is not None:
                            g.wait_ge(sm.h, val)
                        LDx.inc(g.dma_start(out=xring[slot][:], in_=x_t[:, tb, :]), 16)
                    src = xring[slot][:]
                    ldw = (LDx, LDx.n)
                sn = emit_ln(src, Xn[:, tb, :], ld_wait=ldw,
                             scalar_pre=(VE, ve_consts) if tb == 0 else None)
                s_norm.append(sn)
                if tb >= TBM:
                    ring_free[0] = (SE, sn)
            s_ln1_all = SE.n
            tap("ln1", Xn[:], waits=[(SE, s_ln1_all)])

            if stage == "ln1":
                @block.gpsimd
                def _(g):
                    g.wait_ge(LDo.h, LDo.n)
                return nc

            for tb in range(TBM):
                @block.sync
                def _(sy, tb=tb):
                    sy.wait_ge(SE.h, s_norm[tb])
                    TP.inc(sy.dma_start(out=XnT[:, :, tb, :], in_=Xn[:, tb, :],
                                        transpose=True), 16)
                    sy.wait_ge(TP.h, TP.n)
            tp_xnt = TP.n
            tap("xnt", XnT[:], waits=[(TP, tp_xnt)])

            if stage == "xnt":
                @block.gpsimd
                def _(g):
                    g.wait_ge(LDo.h, LDo.n)
                return nc

            accw = {}
            asb_free = [(None, 0)]
            atsl_free = [(None, 0)]
            tp_at = {}
            for g_r in range(4):
                for ibl in range(4):
                    gi = 4 * g_r + ibl
                    par = gi % 2
                    ps = pAR[par]

                    @block.tensor
                    def _(t, gi=gi, ps=ps, gate=tuple(arena_free[par])):
                        if gi == 0:
                            t.wait_ge(SE.h, s_ln1_all)
                        sm, val = gate
                        if sm is not None:
                            t.wait_ge(sm.h, val)
                        for ct in range(TB):
                            for jr in range(4):
                                ins = t.matmul(ps[:, jr * 512:(jr + 1) * 512],
                                               Xn[:, ct, gi * P:(gi + 1) * P],
                                               Xn[:, ct, jr * 512:(jr + 1) * 512],
                                               start=(ct == 0), stop=(ct == TB - 1),
                                               skip_group_check=True)
                        PE.inc(ins)
                    pe_gram = PE.n

                    if stage == "gram0" and gi == 0:
                        @block.vector
                        def _(v, ps=ps, pe_gram=pe_gram):
                            v.wait_ge(PE.h, pe_gram)
                            VE.inc(v.tensor_copy(sqscr[:], ps[:]))
                        tap("G0", sqscr[:], waits=[(VE, VE.n)])

                        @block.gpsimd
                        def _(g):
                            g.wait_ge(LDo.h, LDo.n)
                        return nc

                    mxn = sm_mxn[:, par:par + 1]
                    rs = sm_rs[:, par:par + 1]
                    lnr = sm_lnr[:, par:par + 1]
                    b2 = sm_b2[:, par:par + 1]

                    @block.vector
                    def _(v, ps=ps, pe_gram=pe_gram, mxn=mxn):
                        v.wait_ge(PE.h, pe_gram)
                        VE.inc(v.tensor_reduce(mxn, ps[:], AX.X, OP.max))
                    v_mx = VE.n

                    @block.vector
                    def _(v, v_mx=v_mx, mxn=mxn):
                        v.wait_ge(VE.h, v_mx)
                        VE.inc(v.tensor_scalar_mul(mxn, mxn, -1.0))
                    v_mxn = VE.n

                    @block.scalar
                    def _(s, gi=gi, ps=ps, v_mxn=v_mxn, mxn=mxn, rs=rs,
                          gate=tuple(asb_free[0])):
                        s.wait_ge(VE.h, v_mxn)
                        sm, val = gate
                        if sm is not None:
                            s.wait_ge(sm.h, val)
                        SE.inc(s.activation(out=Asb[0][:], in_=ps[:], func=AF.Exp,
                                            bias=mxn, accum_out=rs))
                    s_e1 = SE.n

                    @block.scalar
                    def _(s, s_e1=s_e1, rs=rs, lnr=lnr):
                        s.wait_ge(SE.h, s_e1)
                        SE.inc(s.activation(out=lnr, in_=rs, func=AF.Ln))
                    s_ln = SE.n

                    @block.vector
                    def _(v, s_ln=s_ln, mxn=mxn, lnr=lnr, b2=b2):
                        v.wait_ge(SE.h, s_ln)
                        VE.inc(v.tensor_tensor(b2, mxn, lnr, OP.subtract))
                    v_b2 = VE.n

                    @block.scalar
                    def _(s, gi=gi, ps=ps, v_b2=v_b2, b2=b2):
                        s.wait_ge(VE.h, v_b2)
                        SE.inc(s.activation(out=Asb[0][:], in_=ps[:], func=AF.Exp,
                                            bias=b2))
                    s_e2 = SE.n
                    arena_free[par] = (SE, s_e2)

                    if stage == "gram1" and gi == 0:
                        tap("A0", Asb[0][:], waits=[(SE, s_e2)])

                        @block.gpsimd
                        def _(g):
                            g.wait_ge(LDo.h, LDo.n)
                        return nc

                    @block.sync
                    def _(sy, g_r=g_r, ibl=ibl, gi=gi, s_e2=s_e2,
                          gate=tuple(atsl_free[0])):
                        sy.wait_ge(SE.h, s_e2)
                        sm, val = gate
                        if sm is not None and ibl == 0:
                            sy.wait_ge(sm.h, val)
                        TP.inc(sy.dma_start(out=ATsl[0][:, :, ibl, :],
                                            in_=Asb[0][:], transpose=True), 16)
                        sy.wait_ge(TP.h, TP.n)
                    asb_free[0] = (TP, TP.n)

                    if stage == "at1" and gi == 0:
                        tap("AT0", ATsl[0][:], waits=[(TP, TP.n)])

                        @block.gpsimd
                        def _(g):
                            g.wait_ge(LDo.h, LDo.n)
                        return nc
                tp_at[g_r] = TP.n

                for tb in range(TBM):
                    par = tb % 2
                    slot = pAR[par][:, 0:512]

                    @block.tensor
                    def _(t, g_r=g_r, tb=tb, slot=slot, gate=tuple(arena_free[par])):
                        t.wait_ge(TP.h, tp_at[g_r])
                        if g_r == 0 and tb == 0:
                            t.wait_ge(TP.h, tp_xnt)
                        sm, val = gate
                        if sm is not None:
                            t.wait_ge(sm.h, val)
                        for m in range(DC):
                            ins = t.matmul(slot, XnT[:, m, tb, :],
                                           flat2(ATsl[0][:, m, :, :]),
                                           start=(m == 0), stop=(m == DC - 1))
                        PE.inc(ins)
                    pe_out = PE.n

                    @block.vector
                    def _(v, g_r=g_r, tb=tb, slot=slot, pe_out=pe_out):
                        v.wait_ge(PE.h, pe_out)
                        a_sl = acc[:, tb, g_r * 512:(g_r + 1) * 512]
                        VE.inc(v.tensor_tensor(a_sl, a_sl, slot, OP.add))
                    arena_free[par] = (VE, VE.n)
                    accw[(g_r, tb)] = VE.n
                atsl_free[0] = (PE, PE.n)
            ve_x1 = VE.n
            tap("x1", acc[:], waits=[(VE, ve_x1)])


        if stage == "attn":
            @block.gpsimd
            def _(g):
                g.wait_ge(LDo.h, LDo.n)
            return nc
        with ExitStack() as ffn_scope:
            ef = ffn_scope.enter_context
            xfT = ef(nc.sbuf_tensor("xfT", [P, DC, TBM, P], BF16))
            xfring = [ef(nc.sbuf_tensor("xfring0", [P, D], BF16))]
            scores = ef(nc.sbuf_tensor("scores", [P, TBM, NE], F32))
            topv = ef(nc.sbuf_tensor("topv", [P, TBM, NE], F32))
            topi = ef(nc.sbuf_tensor("topi", [P, TBM, NE], U32))
            topf = ef(nc.sbuf_tensor("topf", [P, TBM, NE], F32))
            Ssc = ef(nc.sbuf_tensor("Ssc", [P, TBM, NE], F32))
            STb = ef(nc.sbuf_tensor("STb", [NE, TBM * P], F32))
            Sb = ef(nc.sbuf_tensor("Sb", [P, NE, H], BF16))
            natw = [ef(nc.sbuf_tensor(f"natw{i}", [P, D], BF16)) for i in range(NATW)]
            W1T = ef(nc.sbuf_tensor("W1T", [P, DC, W1WIN, P], BF16))
            W3T = ef(nc.sbuf_tensor("W3T", [P, DC, W1WIN, P], BF16))
            W2T = ef(nc.sbuf_tensor("W2T", [P, UB, W2WIN, P], BF16))
            hbuf = ef(nc.sbuf_tensor("hbuf", [P, UB, H], BF16))
            tmp1 = ef(nc.sbuf_tensor("tmp1", [P, 512], F32))
            tmp3 = ef(nc.sbuf_tensor("tmp3", [P, 512], F32))
            silb = ef(nc.sbuf_tensor("silb", [P, 512], F32))

            xf_free = [(None, 0)]
            for tb in range(TBM):
                slot = 0
                gate = xf_free[slot]
                if gate[0] is not None:
                    @block.scalar
                    def _(s, gate=tuple(gate)):
                        s.wait_ge(gate[0].h, gate[1])
                sn = emit_ln(acc[:, tb, :], xfring[slot][:],
                             scalar_pre=(VE, ve_x1) if tb == 0 else None,
                             vector_pre=(VE, ve_x1) if tb == 0 else None)

                @block.sync
                def _(sy, tb=tb, slot=slot, sn=sn):
                    sy.wait_ge(SE.h, sn)
                    TP.inc(sy.dma_start(out=xfT[:, :, tb, :], in_=xfring[slot][:],
                                        transpose=True), 16)
                    sy.wait_ge(TP.h, TP.n)
                xf_free[slot] = (TP, TP.n)
            tp_xfT = TP.n

            gate_arena0 = tuple(arena_free[0])
            gate_arena1 = tuple(arena_free[1])
            pe_gate = []
            s_gate_evac = []
            for tb in range(TBM):
                @block.tensor
                def _(t, tb=tb):
                    t.wait_ge(VE.h, ve_consts)
                    t.wait_ge(TP.h, tp_xfT)
                    if tb < 2:
                        sm, val = gate_arena0
                        if sm is not None:
                            t.wait_ge(sm.h, val)
                    if tb >= 2:
                        t.wait_ge(SE.h, s_gate_evac[tb - 2])
                    for m in range(DC):
                        ins = t.matmul(pAR[0][:, (tb % 2) * 512:(tb % 2) * 512 + NE],
                                       xfT[:, m, tb, :], gT[:, m, 0:NE],
                                       start=(m == 0), stop=(m == DC - 1))
                    PE.inc(ins)
                pe_gate.append(PE.n)

                @block.scalar
                def _(s, tb=tb):
                    s.wait_ge(PE.h, pe_gate[tb])
                    SE.inc(s.activation(out=scores[:, tb, :],
                                        in_=pAR[0][:, (tb % 2) * 512:(tb % 2) * 512 + NE],
                                        func=AF.Sigmoid))
                s_gate_evac.append(SE.n)
            s_scores = SE.n
            tap("scores", scores[:], waits=[(SE, s_scores)])

            if stage == "scores":
                @block.gpsimd
                def _(g):
                    g.wait_ge(LDo.h, LDo.n)
                return nc

            @block.vector
            def _(v):
                v.wait_ge(SE.h, s_scores)
                for tb in range(TBM):
                    VE.inc(v.max(topv[:, tb, :], scores[:, tb, :]))
            v_max = VE.n

            @block.vector
            def _(v):
                v.wait_ge(VE.h, v_max)
                for tb in range(TBM):
                    VE.inc(v.max_index(topi[:, tb, :], topv[:, tb, :], scores[:, tb, :]))
            v_mi = VE.n

            @block.vector
            def _(v):
                v.wait_ge(VE.h, v_mi)
                v.wait_ge(GP.h, 1)
                for tb in range(TBM):
                    VE.inc(v.tensor_copy(topf[:, tb, :], topi[:, tb, :]))
            v_tf = VE.n

            e3t = ef(nc.sbuf_tensor("e3t", [P, TBM, NE], F32))

            @block.vector
            def _(v):
                v.wait_ge(VE.h, v_tf)
                for tb in range(TBM):
                    VE.inc(v.tensor_scalar(Ssc[:, tb, :], iota8[:], topf[:, tb, 0:1],
                                           None, OP.is_equal))
                    VE.inc(v.tensor_scalar(e3t[:, tb, :], iota8[:], topf[:, tb, 1:2],
                                           None, OP.is_equal))
            v_eq = VE.n

            @block.vector
            def _(v):
                v.wait_ge(VE.h, v_eq)
                for tb in range(TBM):
                    VE.inc(v.tensor_scalar(Ssc[:, tb, :], Ssc[:, tb, :], topv[:, tb, 0:1],
                                           None, OP.mult))
            v_s1 = VE.n

            @block.vector
            def _(v):
                v.wait_ge(VE.h, v_s1)
                for tb in range(TBM):
                    VE.inc(v.scalar_tensor_tensor(Ssc[:, tb, :], e3t[:, tb, :],
                                                  topv[:, tb, 1:2], Ssc[:, tb, :],
                                                  OP.mult, OP.add))
            v_S = VE.n
            tap("S", Ssc[:], waits=[(VE, v_S)])

            ve_st = []
            for tb in range(TBM):
                @block.tensor
                def _(t, tb=tb):
                    t.wait_ge(VE.h, v_S)
                    t.wait_ge(VE.h, ve_consts)
                    if tb == 0:
                        sm, val = gate_arena1
                        if sm is not None:
                            t.wait_ge(sm.h, val)
                    else:
                        t.wait_ge(VE.h, ve_st[tb - 1])
                    PE.inc(t.transpose(pAR[1][0:NE, 0:P], Ssc[:, tb, :], identt[:]))
                pe_st = PE.n

                @block.vector
                def _(v, tb=tb, pe_st=pe_st):
                    v.wait_ge(PE.h, pe_st)
                    VE.inc(v.tensor_copy(STb[0:NE, tb * P:(tb + 1) * P],
                                         pAR[1][0:NE, 0:P]))
                ve_st.append(VE.n)
            v_stb = VE.n

            ve_sb_evac = []
            for ex in range(NE):
                @block.tensor
                def _(t, ex=ex):
                    t.wait_ge(VE.h, v_stb)
                    if ex >= 1:
                        t.wait_ge(VE.h, ve_sb_evac[ex - 1])
                    t.matmul(pAR[0][:, 0:512], selt[:, ex * P:(ex + 1) * P],
                             STb[0:NE, 0:512], start=True, stop=True)
                    ins = t.matmul(pAR[0][:, 512:1024], selt[:, ex * P:(ex + 1) * P],
                                   STb[0:NE, 512:1024], start=True, stop=True)
                    PE.inc(ins)
                pe_sb = PE.n

                @block.vector
                def _(v, ex=ex, pe_sb=pe_sb):
                    v.wait_ge(PE.h, pe_sb)
                    VE.inc(v.tensor_copy(Sb[:, ex, :], pAR[0][:, 0:1024]))
                ve_sb_evac.append(VE.n)
            v_sb_all = VE.n
            tap("Sb", Sb[:], waits=[(VE, v_sb_all)])

            if stage == "sb":
                @block.gpsimd
                def _(g):
                    g.wait_ge(LDo.h, LDo.n)
                return nc

            def wsrc(ex):
                if ex < NE:
                    return (w1.rearrange("E (ub p) d -> E p ub d", p=P)[ex],
                            w3.rearrange("E (ub p) d -> E p ub d", p=P)[ex],
                            w2.rearrange("E (k p) u -> E p k u", p=P)[ex])
                return (ws1.rearrange("(ub p) d -> p ub d", p=P),
                        ws3.rearrange("(ub p) d -> p ub d", p=P),
                        ws2.rearrange("(k p) u -> p k u", p=P))

            nat_free = [(VE, ve_x1)] * NATW
            nat_i = [0]
            w1_ready = {}
            w3_ready = {}
            w2_ready = {}
            w1_slotfree = [(None, 0)] * W1WIN
            w3_slotfree = [(None, 0)] * W1WIN
            w2_slotfree = [(None, 0)] * W2WIN

            def stream_weight(ex, kind, idx):
                slot = nat_i[0] % NATW
                nat_i[0] += 1
                s1, s3, s2 = wsrc(ex)
                if kind == 1:
                    src, dst = s1[:, idx, :], W1T[:, :, idx % W1WIN, :]
                    wfree = tuple(w1_slotfree[idx % W1WIN])
                elif kind == 3:
                    src, dst = s3[:, idx, :], W3T[:, :, idx % W1WIN, :]
                    wfree = tuple(w3_slotfree[idx % W1WIN])
                else:
                    src, dst = s2[:, idx, :], W2T[:, :, idx % W2WIN, :]
                    wfree = tuple(w2_slotfree[idx % W2WIN])
                ncols = src.shape[-1]
                ngate = tuple(nat_free[slot])

                @block.gpsimd
                def _(g, slot=slot, src=src, ncols=ncols, ngate=ngate):
                    sm, val = ngate
                    if sm is not None:
                        g.wait_ge(sm.h, val)
                    LDn[slot].inc(g.dma_start(out=natw[slot][:, 0:ncols], in_=src), 16)
                ld_nat = LDn[slot].n

                @block.sync
                def _(sy, slot=slot, dst=dst, ncols=ncols, ld_nat=ld_nat, wfree=wfree):
                    sy.wait_ge(LDn[slot].h, ld_nat)
                    sm, val = wfree
                    if sm is not None:
                        sy.wait_ge(sm.h, val)
                    TP.inc(sy.dma_start(out=dst, in_=natw[slot][:, 0:ncols],
                                        transpose=True), 16)
                    sy.wait_ge(TP.h, TP.n)
                nat_free[slot] = (TP, TP.n)
                if kind == 1:
                    w1_ready[(ex, idx)] = TP.n
                elif kind == 3:
                    w3_ready[(ex, idx)] = TP.n
                else:
                    w2_ready[(ex, idx)] = TP.n

            arena_free_z = [(VE, v_sb_all), (VE, v_stb)]
            last_sil = [0]
            last_hmul = [0]
            slotfree_r = [(None, 0), (None, 0)]
            h_readers_done = [0]

            for ex in range(NEXP):
                is_shared = (ex == NEXP - 1)
                for ub in range(W1WIN):
                    stream_weight(ex, 1, ub)
                    stream_weight(ex, 3, ub)

                for ub in range(UB):
                    par = ub % 2
                    ps = pAR[par]

                    @block.tensor
                    def _(t, ex=ex, ub=ub, ps=ps, gate=tuple(arena_free_z[par])):
                        t.wait_ge(TP.h, w1_ready[(ex, ub)])
                        t.wait_ge(TP.h, w3_ready[(ex, ub)])
                        sm, val = gate
                        if sm is not None:
                            t.wait_ge(sm.h, val)
                        for m in range(DC):
                            l1 = W1T[:, m, ub % W1WIN, :]
                            l3 = W3T[:, m, ub % W1WIN, :]
                            t.matmul(ps[:, 0:512], l1, flat2(xfT[:, m, 0:4, :]),
                                     start=(m == 0), stop=(m == DC - 1),
                                     skip_group_check=True)
                            t.matmul(ps[:, 512:1024], l1, flat2(xfT[:, m, 4:8, :]),
                                     start=(m == 0), stop=(m == DC - 1),
                                     skip_group_check=True)
                            t.matmul(ps[:, 1024:1536], l3, flat2(xfT[:, m, 0:4, :]),
                                     start=(m == 0), stop=(m == DC - 1),
                                     skip_group_check=True)
                            ins = t.matmul(ps[:, 1536:2048], l3, flat2(xfT[:, m, 4:8, :]),
                                           start=(m == 0), stop=(m == DC - 1),
                                           skip_group_check=True)
                        PE.inc(ins)
                    pe_z = PE.n
                    w1_slotfree[ub % W1WIN] = (PE, pe_z)
                    w3_slotfree[ub % W1WIN] = (PE, pe_z)
                    if ub + W1WIN < UB:
                        stream_weight(ex, 1, ub + W1WIN)
                        stream_weight(ex, 3, ub + W1WIN)
                    elif ub == UB - 1:
                        for k in range(4):
                            stream_weight(ex, 2, k)

                    for th in range(2):
                        z1 = ps[:, th * 512:(th + 1) * 512]
                        z3 = ps[:, 1024 + th * 512:1024 + (th + 1) * 512]
                        h_sl = hbuf[:, ub, th * 512:(th + 1) * 512]
                        if not is_shared:
                            sbs = Sb[:, ex, th * 512:(th + 1) * 512]

                            @block.vector
                            def _(v, pe_z=pe_z, z1=z1, sbs=sbs, ls=last_sil[0]):
                                v.wait_ge(PE.h, pe_z)
                                if ex == 0 and ub == 0 and th == 0:
                                    v.wait_ge(VE.h, v_sb_all)
                                if ls:
                                    v.wait_ge(SE.h, ls)
                                VE.inc(v.tensor_tensor(tmp1[:], z1, sbs, OP.mult))
                            v_t1 = VE.n

                            @block.scalar
                            def _(s, v_t1=v_t1):
                                s.wait_ge(VE.h, v_t1)
                                SE.inc(s.activation(out=silb[:], in_=tmp1[:], func=AF.Silu))
                            s_sil = SE.n
                            last_sil[0] = s_sil

                            @block.vector
                            def _(v, z3=z3, sbs=sbs, h_sl=h_sl, s_sil=s_sil,
                                  hrd=h_readers_done[0], lh=last_hmul[0]):
                                if lh:
                                    v.wait_ge(VE.h, lh)
                                v_t3 = VE.inc(v.tensor_tensor(tmp3[:], z3, sbs, OP.mult))
                                v.wait_ge(SE.h, s_sil)
                                v.wait_ge(PE.h, hrd)
                                v.wait_ge(VE.h, v_t3)
                                VE.inc(v.tensor_tensor(h_sl, silb[:], tmp3[:], OP.mult))
                        else:
                            @block.scalar
                            def _(s, pe_z=pe_z, z1=z1, lh=last_hmul[0]):
                                s.wait_ge(PE.h, pe_z)
                                if lh:
                                    s.wait_ge(VE.h, lh)
                                SE.inc(s.activation(out=silb[:], in_=z1, func=AF.Silu))
                            s_sil = SE.n
                            last_sil[0] = s_sil

                            @block.vector
                            def _(v, z3=z3, h_sl=h_sl, s_sil=s_sil,
                                  hrd=h_readers_done[0]):
                                v.wait_ge(SE.h, s_sil)
                                v.wait_ge(PE.h, hrd)
                                VE.inc(v.tensor_tensor(h_sl, silb[:], z3, OP.mult))
                        last_hmul[0] = VE.n
                    arena_free_z[par] = (VE, VE.n)
                ve_h_all = VE.n

                for kg in range(4):
                    for tb in range(TBM):
                        par = tb % 2
                        slot = pAR[par][:, 0:512]

                        @block.tensor
                        def _(t, ex=ex, kg=kg, tb=tb, slot=slot,
                              gate=tuple(slotfree_r[par]),
                              zgate=tuple(arena_free_z[par]), ve_h_all=ve_h_all):
                            for kk in range(4):
                                t.wait_ge(TP.h, w2_ready[(ex, kg * 4 + kk)])
                            t.wait_ge(VE.h, ve_h_all)
                            sm, val = gate
                            if sm is not None:
                                t.wait_ge(sm.h, val)
                            if kg == 0 and tb < 2:
                                sm2, val2 = zgate
                                if sm2 is not None:
                                    t.wait_ge(sm2.h, val2)
                            base = (kg * 4) % W2WIN
                            for uc in range(UB):
                                ins = t.matmul(slot, hbuf[:, uc, tb * P:(tb + 1) * P],
                                               flat2(W2T[:, uc, base:base + 4, :]),
                                               start=(uc == 0), stop=(uc == UB - 1))
                            PE.inc(ins)
                        pe_r = PE.n
                        if tb == TBM - 1:
                            for kk in range(4):
                                w2_slotfree[(kg * 4 + kk) % W2WIN] = (PE, pe_r)

                        @block.vector
                        def _(v, kg=kg, tb=tb, slot=slot, pe_r=pe_r,
                              aw=accw[(kg, tb)]):
                            v.wait_ge(PE.h, pe_r)
                            v.wait_ge(VE.h, aw)
                            a_sl = acc[:, tb, kg * 512:(kg + 1) * 512]
                            VE.inc(v.tensor_tensor(a_sl, a_sl, slot, OP.add))
                        slotfree_r[par] = (VE, VE.n)
                        accw[(kg, tb)] = VE.n
                    if kg < 3:
                        for k in range((kg + 1) * 4, (kg + 2) * 4):
                            stream_weight(ex, 2, k)
                h_readers_done[0] = PE.n
                arena_free_z = [(VE, VE.n), (VE, VE.n)]
            ve_final = VE.n

            @block.gpsimd
            def _(g):
                g.wait_ge(VE.h, ve_final)
                LDo.inc(g.dma_start(out=out_t[:], in_=acc[:]), 16)
                g.wait_ge(LDo.h, LDo.n)

    return nc


def _host_inputs(L0_attn_x, L0_ffn_gate_w, L0_ffn_w1, L0_ffn_w2, L0_ffn_w3,
                 L0_ffn_ws1, L0_ffn_ws2, L0_ffn_ws3):
    xf = np.ascontiguousarray(np.asarray(L0_attn_x, dtype=np.float32))
    gw = np.ascontiguousarray(np.asarray(L0_ffn_gate_w, dtype=np.float32))
    w1 = np.ascontiguousarray(np.asarray(L0_ffn_w1, dtype=np.float32))
    w2 = np.ascontiguousarray(np.asarray(L0_ffn_w2, dtype=np.float32))
    w3 = np.ascontiguousarray(np.asarray(L0_ffn_w3, dtype=np.float32))
    ws1 = np.ascontiguousarray(np.asarray(L0_ffn_ws1, dtype=np.float32))
    ws2 = np.ascontiguousarray(np.asarray(L0_ffn_ws2, dtype=np.float32))
    ws3 = np.ascontiguousarray(np.asarray(L0_ffn_ws3, dtype=np.float32))
    gwT = np.ascontiguousarray(gw.T)
    in_maps = []
    for c in range(8):
        b, h = c // 2, c % 2
        mine = xf[b, h * 1024:(h + 1) * 1024]
        other = xf[b, (1 - h) * 1024:(2 - h) * 1024]
        xc = np.ascontiguousarray(np.concatenate([mine, other], axis=0))
        in_maps.append(dict(x=xc, gwT=gwT, w1=w1, w2=w2, w3=w3,
                            ws1=ws1, ws2=ws2, ws3=ws3))
    return in_maps


def kernel(L0_attn_x, L0_ffn_gate_w, L0_ffn_w1, L0_ffn_w2, L0_ffn_w3,
           L0_ffn_ws1, L0_ffn_ws2, L0_ffn_ws3):
    in_maps = _host_inputs(L0_attn_x, L0_ffn_gate_w, L0_ffn_w1, L0_ffn_w2,
                           L0_ffn_w3, L0_ffn_ws1, L0_ffn_ws2, L0_ffn_ws3)
    nc = build()
    res = run_bass_kernel_spmd(nc, in_maps, core_ids=list(range(8)))
    out = np.empty((4, 2048, 2048), dtype=np.float32)
    for c in range(8):
        b, h = c // 2, c % 2
        out[b, h * 1024:(h + 1) * 1024] = res.results[c]["out"]
    return out


# version 11
# speedup vs baseline: 1.7109x; 1.7109x over previous
import numpy as np
from contextlib import ExitStack

import concourse.bass as bass
import concourse.mybir as mybir
from concourse.bass_utils import run_bass_kernel_spmd
from concourse.alu_op_type import AluOpType as OP

F32 = mybir.dt.float32
BF16 = mybir.dt.bfloat16
U32 = mybir.dt.uint32
AF = mybir.ActivationFunctionType
AX = mybir.AxisListType

P = 128
D = 2048
SEQ = 2048
TB = 16
TBM = 8
DC = 16
H = 1024
UB = 8
NE = 8
NEXP = 9
EPS = 1e-5
INV_D = 1.0 / D

NATW = 2
W1WIN = 4
W2WIN = 4


class Sem:
    def __init__(self, h):
        self.h = h
        self.n = 0

    def inc(self, instr, k=1):
        instr.then_inc(self.h, k)
        self.n += k
        return self.n


def flat2(ap):
    return ap.rearrange("p a b -> p (a b)")


def build(want_taps=(), stage="full"):
    nc = bass.Bass("TRN2", target_bir_lowering=False, debug=False)

    x = nc.dram_tensor("x", [SEQ, D], F32, kind="ExternalInput").ap()
    gwT = nc.dram_tensor("gwT", [D, NE], F32, kind="ExternalInput").ap()
    w1t = nc.dram_tensor("w1t", [NE, D, H], F32, kind="ExternalInput").ap()
    w2t = nc.dram_tensor("w2t", [NE, H, D], F32, kind="ExternalInput").ap()
    w3t = nc.dram_tensor("w3t", [NE, D, H], F32, kind="ExternalInput").ap()
    ws1t = nc.dram_tensor("ws1t", [D, H], F32, kind="ExternalInput").ap()
    ws2t = nc.dram_tensor("ws2t", [H, D], F32, kind="ExternalInput").ap()
    ws3t = nc.dram_tensor("ws3t", [D, H], F32, kind="ExternalInput").ap()
    out = nc.dram_tensor("out", [TBM * P, D], F32, kind="ExternalOutput").ap()

    x_t = x.rearrange("(tb p) d -> p tb d", p=P)
    out_t = out.rearrange("(tb p) d -> p tb d", p=P)

    with ExitStack() as top:
        e = top.enter_context
        acc = e(nc.sbuf_tensor("acc", [P, TBM, D], F32))
        identt = e(nc.sbuf_tensor("identt", [P, P], F32))
        selt = e(nc.sbuf_tensor("selt", [NE, NE * P], F32))
        iota8 = e(nc.sbuf_tensor("iota8", [P, NE], F32))
        gT = e(nc.sbuf_tensor("gT", [P, DC, NE], BF16))
        gws = e(nc.sbuf_tensor("gws", [P, DC, NE], F32))
        st_sum = e(nc.sbuf_tensor("st_sum", [P, 1], F32))
        st_ssq = e(nc.sbuf_tensor("st_ssq", [P, 1], F32))
        st_mean = e(nc.sbuf_tensor("st_mean", [P, 1], F32))
        st_var = e(nc.sbuf_tensor("st_var", [P, 1], F32))
        st_std = e(nc.sbuf_tensor("st_std", [P, 1], F32))
        st_rstd = e(nc.sbuf_tensor("st_rstd", [P, 1], F32))
        st_nmr = e(nc.sbuf_tensor("st_nmr", [P, 1], F32))
        st_tmp = e(nc.sbuf_tensor("st_tmp", [P, 1], F32))
        sqscr = e(nc.sbuf_tensor("sqscr", [P, D], F32))
        sm_mxn = e(nc.sbuf_tensor("sm_mxn", [P, 2], F32))
        sm_rs = e(nc.sbuf_tensor("sm_rs", [P, 2], F32))
        sm_lnr = e(nc.sbuf_tensor("sm_lnr", [P, 2], F32))
        sm_b2 = e(nc.sbuf_tensor("sm_b2", [P, 2], F32))

        pA = e(nc.psum_tensor("pA", [P, 512 * 4], F32))
        pB = e(nc.psum_tensor("pB", [P, 512 * 4], F32))
        pAR = [pA, pB]

        LDa = Sem(e(nc.semaphore("LDa")))
        LDx = Sem(e(nc.semaphore("LDx")))
        LDg = Sem(e(nc.semaphore("LDg")))
        LDw1 = [Sem(e(nc.semaphore(f"LDw1_{i}"))) for i in range(2)]
        LDw3 = [Sem(e(nc.semaphore(f"LDw3_{i}"))) for i in range(2)]
        LDw2 = [Sem(e(nc.semaphore(f"LDw2_{i}"))) for i in range(2)]
        LDo = Sem(e(nc.semaphore("LDo")))
        GP = Sem(e(nc.semaphore("GP")))
        TP = Sem(e(nc.semaphore("TP")))
        PE = Sem(e(nc.semaphore("PE")))
        VE = Sem(e(nc.semaphore("VE")))
        SE = Sem(e(nc.semaphore("SE")))

        block = e(nc.Block())

        def tap(name, ap, waits=()):
            if name not in want_taps:
                return
            t = nc.dram_tensor("tap_" + name, list(ap.shape), ap.dtype,
                               kind="ExternalOutput").ap()

            @block.gpsimd
            def _(g, ap=ap, t=t, waits=tuple(waits)):
                for s, v in waits:
                    g.wait_ge(s.h, v)
                LDo.inc(g.dma_start(out=t, in_=ap), 16)

        @block.gpsimd
        def _(g):
            GP.inc(g.iota(iota8[:], pattern=[[1, NE]], base=0, channel_multiplier=0,
                          allow_small_or_imprecise_dtypes=True), 1)
            GP.inc(g.iota(identt[:], pattern=[[0, P]], base=0, channel_multiplier=1,
                          allow_small_or_imprecise_dtypes=True), 1)
            GP.inc(g.iota(sqscr[:, 0:P], pattern=[[1, P]], base=0, channel_multiplier=0,
                          allow_small_or_imprecise_dtypes=True), 1)
            GP.inc(g.iota(selt[:], pattern=[[0, NE * P]], base=0, channel_multiplier=1,
                          allow_small_or_imprecise_dtypes=True), 1)
            GP.inc(g.iota(sqscr[0:NE, P:P + NE * P].rearrange("k (e j) -> k e j", j=P),
                          pattern=[[1, NE], [0, P]], base=0, channel_multiplier=0,
                          allow_small_or_imprecise_dtypes=True), 1)
            LDg.inc(g.dma_start(out=gws[:], in_=gwT.rearrange("(m p) e -> p m e", p=P)), 16)
            LDa.inc(g.dma_start(out=acc[:], in_=x_t[:, 0:TBM, :]), 16)
        gp_consts = GP.n
        ld_gws = (LDg, LDg.n)
        ld_acc = (LDa, LDa.n)

        @block.vector
        def _(v):
            v.wait_ge(GP.h, gp_consts)
            v.wait_ge(ld_gws[0].h, ld_gws[1])
            VE.inc(v.tensor_tensor(identt[:], identt[:], sqscr[:, 0:P], OP.is_equal))
            VE.inc(v.tensor_tensor(selt[:], selt[:], sqscr[0:NE, P:P + NE * P], OP.is_equal))
            VE.inc(v.tensor_copy(gT[:], gws[:]))
        ve_consts = VE.n

        st_m2 = e(nc.sbuf_tensor("st_m2", [P, 1], F32))

        def emit_ln(src_ap, dst_ap, ld_wait=None, scalar_pre=None, vector_pre=None):
            @block.vector
            def _(v):
                if ld_wait is not None:
                    v.wait_ge(ld_wait[0].h, ld_wait[1])
                if vector_pre is not None:
                    v.wait_ge(vector_pre[0].h, vector_pre[1])
                VE.inc(v.tensor_reduce(st_sum[:], src_ap, AX.X, OP.add, negate=True))
            v_red = VE.n

            @block.scalar
            def _(s):
                if ld_wait is not None:
                    s.wait_ge(ld_wait[0].h, ld_wait[1])
                if scalar_pre is not None:
                    s.wait_ge(scalar_pre[0].h, scalar_pre[1])
                SE.inc(s.activation(out=sqscr[:], in_=src_ap, func=AF.Square,
                                    accum_out=st_ssq[:]))
            s_sq = SE.n

            @block.vector
            def _(v):
                v.wait_ge(VE.h, v_red)
                v.wait_ge(SE.h, s_sq)
                VE.inc(v.tensor_scalar_mul(st_mean[:], st_sum[:], INV_D))
                VE.inc(v.tensor_scalar_mul(st_tmp[:], st_ssq[:], INV_D))
            v_d1 = VE.n

            @block.vector
            def _(v):
                v.wait_ge(VE.h, v_d1)
                VE.inc(v.tensor_tensor(st_m2[:], st_mean[:], st_mean[:], OP.mult))
            v_d2 = VE.n

            @block.vector
            def _(v):
                v.wait_ge(VE.h, v_d2)
                VE.inc(v.tensor_tensor(st_var[:], st_tmp[:], st_m2[:], OP.subtract))
            v_d3 = VE.n

            @block.vector
            def _(v):
                v.wait_ge(VE.h, v_d3)
                VE.inc(v.tensor_scalar_add(st_var[:], st_var[:], EPS))
            v_d4 = VE.n

            @block.scalar
            def _(s):
                s.wait_ge(VE.h, v_d4)
                SE.inc(s.activation(out=st_std[:], in_=st_var[:], func=AF.Sqrt))
            s_std = SE.n

            @block.vector
            def _(v):
                v.wait_ge(SE.h, s_std)
                VE.inc(v.reciprocal(st_rstd[:], st_std[:]))
            v_rec = VE.n

            @block.vector
            def _(v):
                v.wait_ge(VE.h, v_rec)
                VE.inc(v.tensor_tensor(st_nmr[:], st_mean[:], st_rstd[:], OP.mult))
            v_nmr = VE.n

            @block.scalar
            def _(s):
                s.wait_ge(VE.h, v_nmr)
                SE.inc(s.activation(out=dst_ap, in_=src_ap, func=AF.Identity,
                                    bias=st_nmr[:], scale=st_rstd[:]))
            return SE.n

        arena_free = [(None, 0), (None, 0)]
        with ExitStack() as attn_scope:
            ea = attn_scope.enter_context
            Xn = ea(nc.sbuf_tensor("Xn", [P, TB, D], BF16))
            XnT = ea(nc.sbuf_tensor("XnT", [P, DC, TBM, P], BF16))
            xring = [ea(nc.sbuf_tensor("xring0", [P, D], F32))]
            Asb = [ea(nc.sbuf_tensor("Asb0", [P, D], BF16))]
            ATsl = [ea(nc.sbuf_tensor("ATsl0", [P, DC, 4, P], BF16))]

            s_norm = []
            ring_free = [(None, 0), (None, 0)]
            for tb in range(TB):
                if tb < TBM:
                    src = acc[:, tb, :]
                    ldw = ld_acc
                else:
                    slot = 0

                    @block.gpsimd
                    def _(g, slot=slot, tb=tb, gate=tuple(ring_free[slot])):
                        sm, val = gate
                        if sm is not None:
                            g.wait_ge(sm.h, val)
                        LDx.inc(g.dma_start(out=xring[slot][:], in_=x_t[:, tb, :]), 16)
                    src = xring[slot][:]
                    ldw = (LDx, LDx.n)
                sn = emit_ln(src, Xn[:, tb, :], ld_wait=ldw,
                             scalar_pre=(VE, ve_consts) if tb == 0 else None)
                s_norm.append(sn)
                if tb >= TBM:
                    ring_free[0] = (SE, sn)
            s_ln1_all = SE.n
            tap("ln1", Xn[:], waits=[(SE, s_ln1_all)])

            if stage == "ln1":
                @block.gpsimd
                def _(g):
                    g.wait_ge(LDo.h, LDo.n)
                return nc

            for tb in range(TBM):
                @block.sync
                def _(sy, tb=tb):
                    sy.wait_ge(SE.h, s_norm[tb])
                    TP.inc(sy.dma_start(out=XnT[:, :, tb, :], in_=Xn[:, tb, :],
                                        transpose=True), 16)
                    sy.wait_ge(TP.h, TP.n)
            tp_xnt = TP.n
            tap("xnt", XnT[:], waits=[(TP, tp_xnt)])

            if stage == "xnt":
                @block.gpsimd
                def _(g):
                    g.wait_ge(LDo.h, LDo.n)
                return nc

            accw = {}
            asb_free = [(None, 0)]
            atsl_free = [(None, 0)]
            tp_at = {}
            for g_r in range(4):
                for ibl in range(4):
                    gi = 4 * g_r + ibl
                    par = gi % 2
                    ps = pAR[par]

                    @block.tensor
                    def _(t, gi=gi, ps=ps, gate=tuple(arena_free[par])):
                        if gi == 2:
                            t.wait_ge(SE.h, s_ln1_all)
                        sm, val = gate
                        if sm is not None:
                            t.wait_ge(sm.h, val)
                        for ct in range(TB):
                            if gi < 2:
                                t.wait_ge(SE.h, s_norm[ct])
                            for jr in range(4):
                                ins = t.matmul(ps[:, jr * 512:(jr + 1) * 512],
                                               Xn[:, ct, gi * P:(gi + 1) * P],
                                               Xn[:, ct, jr * 512:(jr + 1) * 512],
                                               start=(ct == 0), stop=(ct == TB - 1),
                                               skip_group_check=True)
                        PE.inc(ins)
                    pe_gram = PE.n

                    if stage == "gram0" and gi == 0:
                        @block.vector
                        def _(v, ps=ps, pe_gram=pe_gram):
                            v.wait_ge(PE.h, pe_gram)
                            VE.inc(v.tensor_copy(sqscr[:], ps[:]))
                        tap("G0", sqscr[:], waits=[(VE, VE.n)])

                        @block.gpsimd
                        def _(g):
                            g.wait_ge(LDo.h, LDo.n)
                        return nc

                    mxn = sm_mxn[:, par:par + 1]
                    rs = sm_rs[:, par:par + 1]
                    lnr = sm_lnr[:, par:par + 1]
                    b2 = sm_b2[:, par:par + 1]

                    @block.vector
                    def _(v, ps=ps, pe_gram=pe_gram, mxn=mxn):
                        v.wait_ge(PE.h, pe_gram)
                        VE.inc(v.tensor_reduce(mxn, ps[:], AX.X, OP.max, negate=True))
                    v_mxn = VE.n

                    @block.scalar
                    def _(s, gi=gi, ps=ps, v_mxn=v_mxn, mxn=mxn, rs=rs,
                          gate=tuple(asb_free[0])):
                        s.wait_ge(VE.h, v_mxn)
                        sm, val = gate
                        if sm is not None:
                            s.wait_ge(sm.h, val)
                        SE.inc(s.activation(out=Asb[0][:], in_=ps[:], func=AF.Exp,
                                            bias=mxn, accum_out=rs))
                    s_e1 = SE.n

                    @block.scalar
                    def _(s, s_e1=s_e1, rs=rs, lnr=lnr):
                        s.wait_ge(SE.h, s_e1)
                        SE.inc(s.activation(out=lnr, in_=rs, func=AF.Ln))
                    s_ln = SE.n

                    @block.vector
                    def _(v, s_ln=s_ln, mxn=mxn, lnr=lnr, b2=b2):
                        v.wait_ge(SE.h, s_ln)
                        VE.inc(v.tensor_tensor(b2, mxn, lnr, OP.subtract))
                    v_b2 = VE.n

                    @block.scalar
                    def _(s, gi=gi, ps=ps, v_b2=v_b2, b2=b2):
                        s.wait_ge(VE.h, v_b2)
                        SE.inc(s.activation(out=Asb[0][:], in_=ps[:], func=AF.Exp,
                                            bias=b2))
                    s_e2 = SE.n
                    arena_free[par] = (SE, s_e2)

                    if stage == "gram1" and gi == 0:
                        tap("A0", Asb[0][:], waits=[(SE, s_e2)])

                        @block.gpsimd
                        def _(g):
                            g.wait_ge(LDo.h, LDo.n)
                        return nc

                    @block.sync
                    def _(sy, g_r=g_r, ibl=ibl, gi=gi, s_e2=s_e2,
                          gate=tuple(atsl_free[0])):
                        sy.wait_ge(SE.h, s_e2)
                        sm, val = gate
                        if sm is not None and ibl == 0:
                            sy.wait_ge(sm.h, val)
                        TP.inc(sy.dma_start(out=ATsl[0][:, :, ibl, :],
                                            in_=Asb[0][:], transpose=True), 16)
                        sy.wait_ge(TP.h, TP.n)
                    asb_free[0] = (TP, TP.n)

                    if stage == "at1" and gi == 0:
                        tap("AT0", ATsl[0][:], waits=[(TP, TP.n)])

                        @block.gpsimd
                        def _(g):
                            g.wait_ge(LDo.h, LDo.n)
                        return nc
                tp_at[g_r] = TP.n

                for tb in range(TBM):
                    par = tb % 2
                    slot = pAR[par][:, 0:512]

                    @block.tensor
                    def _(t, g_r=g_r, tb=tb, slot=slot, gate=tuple(arena_free[par])):
                        t.wait_ge(TP.h, tp_at[g_r])
                        if g_r == 0 and tb == 0:
                            t.wait_ge(TP.h, tp_xnt)
                        sm, val = gate
                        if sm is not None:
                            t.wait_ge(sm.h, val)
                        for m in range(DC):
                            ins = t.matmul(slot, XnT[:, m, tb, :],
                                           flat2(ATsl[0][:, m, :, :]),
                                           start=(m == 0), stop=(m == DC - 1))
                        PE.inc(ins)
                    pe_out = PE.n

                    @block.vector
                    def _(v, g_r=g_r, tb=tb, slot=slot, pe_out=pe_out):
                        v.wait_ge(PE.h, pe_out)
                        a_sl = acc[:, tb, g_r * 512:(g_r + 1) * 512]
                        VE.inc(v.tensor_tensor(a_sl, a_sl, slot, OP.add))
                    arena_free[par] = (VE, VE.n)
                    accw[(g_r, tb)] = VE.n
                atsl_free[0] = (PE, PE.n)
            ve_x1 = VE.n
            tap("x1", acc[:], waits=[(VE, ve_x1)])


        if stage == "attn":
            @block.gpsimd
            def _(g):
                g.wait_ge(LDo.h, LDo.n)
            return nc
        with ExitStack() as ffn_scope:
            ef = ffn_scope.enter_context
            xfT = ef(nc.sbuf_tensor("xfT", [P, DC, TBM, P], BF16))
            xfring = [ef(nc.sbuf_tensor("xfring0", [P, D], BF16))]
            scores = ef(nc.sbuf_tensor("scores", [P, TBM, NE], F32))
            topv = ef(nc.sbuf_tensor("topv", [P, TBM, NE], F32))
            topi = ef(nc.sbuf_tensor("topi", [P, TBM, NE], U32))
            topf = ef(nc.sbuf_tensor("topf", [P, TBM, NE], F32))
            Ssc = ef(nc.sbuf_tensor("Ssc", [P, TBM, NE], F32))
            STb = ef(nc.sbuf_tensor("STb", [NE, TBM * P], F32))
            Sb = ef(nc.sbuf_tensor("Sb", [P, NE, H], BF16))
            W1T = ef(nc.sbuf_tensor("W1T", [P, DC, 2, 256], BF16))
            W3T = ef(nc.sbuf_tensor("W3T", [P, DC, 2, 256], BF16))
            W2T = ef(nc.sbuf_tensor("W2T", [P, UB, 2, 512], BF16))
            hbuf = ef(nc.sbuf_tensor("hbuf", [P, UB, H], BF16))
            tmp1 = ef(nc.sbuf_tensor("tmp1", [P, 512], F32))
            tmp3 = ef(nc.sbuf_tensor("tmp3", [P, 512], F32))
            silb = ef(nc.sbuf_tensor("silb", [P, 512], F32))

            xf_free = [(None, 0)]
            for tb in range(TBM):
                slot = 0
                gate = xf_free[slot]
                if gate[0] is not None:
                    @block.scalar
                    def _(s, gate=tuple(gate)):
                        s.wait_ge(gate[0].h, gate[1])
                sn = emit_ln(acc[:, tb, :], xfring[slot][:],
                             scalar_pre=(VE, ve_x1) if tb == 0 else None,
                             vector_pre=(VE, ve_x1) if tb == 0 else None)

                @block.sync
                def _(sy, tb=tb, slot=slot, sn=sn):
                    sy.wait_ge(SE.h, sn)
                    TP.inc(sy.dma_start(out=xfT[:, :, tb, :], in_=xfring[slot][:],
                                        transpose=True), 16)
                    sy.wait_ge(TP.h, TP.n)
                xf_free[slot] = (TP, TP.n)
            tp_xfT = TP.n

            gate_arena0 = tuple(arena_free[0])
            gate_arena1 = tuple(arena_free[1])
            pe_gate = []
            s_gate_evac = []
            for tb in range(TBM):
                @block.tensor
                def _(t, tb=tb):
                    t.wait_ge(VE.h, ve_consts)
                    t.wait_ge(TP.h, tp_xfT)
                    if tb < 2:
                        sm, val = gate_arena0
                        if sm is not None:
                            t.wait_ge(sm.h, val)
                    if tb >= 2:
                        t.wait_ge(SE.h, s_gate_evac[tb - 2])
                    for m in range(DC):
                        ins = t.matmul(pAR[0][:, (tb % 2) * 512:(tb % 2) * 512 + NE],
                                       xfT[:, m, tb, :], gT[:, m, 0:NE],
                                       start=(m == 0), stop=(m == DC - 1))
                    PE.inc(ins)
                pe_gate.append(PE.n)

                @block.scalar
                def _(s, tb=tb):
                    s.wait_ge(PE.h, pe_gate[tb])
                    SE.inc(s.activation(out=scores[:, tb, :],
                                        in_=pAR[0][:, (tb % 2) * 512:(tb % 2) * 512 + NE],
                                        func=AF.Sigmoid))
                s_gate_evac.append(SE.n)
            s_scores = SE.n
            tap("scores", scores[:], waits=[(SE, s_scores)])

            if stage == "scores":
                @block.gpsimd
                def _(g):
                    g.wait_ge(LDo.h, LDo.n)
                return nc

            @block.vector
            def _(v):
                v.wait_ge(SE.h, s_scores)
                for tb in range(TBM):
                    VE.inc(v.max(topv[:, tb, :], scores[:, tb, :]))
            v_max = VE.n

            @block.vector
            def _(v):
                v.wait_ge(VE.h, v_max)
                for tb in range(TBM):
                    VE.inc(v.max_index(topi[:, tb, :], topv[:, tb, :], scores[:, tb, :]))
            v_mi = VE.n

            @block.vector
            def _(v):
                v.wait_ge(VE.h, v_mi)
                v.wait_ge(GP.h, 1)
                for tb in range(TBM):
                    VE.inc(v.tensor_copy(topf[:, tb, :], topi[:, tb, :]))
            v_tf = VE.n

            e3t = ef(nc.sbuf_tensor("e3t", [P, TBM, NE], F32))

            @block.vector
            def _(v):
                v.wait_ge(VE.h, v_tf)
                for tb in range(TBM):
                    VE.inc(v.tensor_scalar(Ssc[:, tb, :], iota8[:], topf[:, tb, 0:1],
                                           None, OP.is_equal))
                    VE.inc(v.tensor_scalar(e3t[:, tb, :], iota8[:], topf[:, tb, 1:2],
                                           None, OP.is_equal))
            v_eq = VE.n

            @block.vector
            def _(v):
                v.wait_ge(VE.h, v_eq)
                for tb in range(TBM):
                    VE.inc(v.tensor_scalar(Ssc[:, tb, :], Ssc[:, tb, :], topv[:, tb, 0:1],
                                           None, OP.mult))
            v_s1 = VE.n

            @block.vector
            def _(v):
                v.wait_ge(VE.h, v_s1)
                for tb in range(TBM):
                    VE.inc(v.scalar_tensor_tensor(Ssc[:, tb, :], e3t[:, tb, :],
                                                  topv[:, tb, 1:2], Ssc[:, tb, :],
                                                  OP.mult, OP.add))
            v_S = VE.n
            tap("S", Ssc[:], waits=[(VE, v_S)])

            ve_st = []
            for tb in range(TBM):
                @block.tensor
                def _(t, tb=tb):
                    t.wait_ge(VE.h, v_S)
                    t.wait_ge(VE.h, ve_consts)
                    if tb == 0:
                        sm, val = gate_arena1
                        if sm is not None:
                            t.wait_ge(sm.h, val)
                    else:
                        t.wait_ge(VE.h, ve_st[tb - 1])
                    PE.inc(t.transpose(pAR[1][0:NE, 0:P], Ssc[:, tb, :], identt[:]))
                pe_st = PE.n

                @block.vector
                def _(v, tb=tb, pe_st=pe_st):
                    v.wait_ge(PE.h, pe_st)
                    VE.inc(v.tensor_copy(STb[0:NE, tb * P:(tb + 1) * P],
                                         pAR[1][0:NE, 0:P]))
                ve_st.append(VE.n)
            v_stb = VE.n

            ve_sb_evac = []
            for ex in range(NE):
                @block.tensor
                def _(t, ex=ex):
                    t.wait_ge(VE.h, v_stb)
                    if ex >= 1:
                        t.wait_ge(VE.h, ve_sb_evac[ex - 1])
                    t.matmul(pAR[0][:, 0:512], selt[:, ex * P:(ex + 1) * P],
                             STb[0:NE, 0:512], start=True, stop=True)
                    ins = t.matmul(pAR[0][:, 512:1024], selt[:, ex * P:(ex + 1) * P],
                                   STb[0:NE, 512:1024], start=True, stop=True)
                    PE.inc(ins)
                pe_sb = PE.n

                @block.vector
                def _(v, ex=ex, pe_sb=pe_sb):
                    v.wait_ge(PE.h, pe_sb)
                    VE.inc(v.tensor_copy(Sb[:, ex, :], pAR[0][:, 0:1024]))
                ve_sb_evac.append(VE.n)
            v_sb_all = VE.n
            tap("Sb", Sb[:], waits=[(VE, v_sb_all)])

            if stage == "sb":
                @block.gpsimd
                def _(g):
                    g.wait_ge(LDo.h, LDo.n)
                return nc

            w1t_r = w1t.rearrange("E (m p) u -> E p m u", p=P)
            w3t_r = w3t.rearrange("E (m p) u -> E p m u", p=P)
            w2t_r = w2t.rearrange("E (mu p) d -> E p mu d", p=P)
            ws1t_r = ws1t.rearrange("(m p) u -> p m u", p=P)
            ws3t_r = ws3t.rearrange("(m p) u -> p m u", p=P)
            ws2t_r = ws2t.rearrange("(mu p) d -> p mu d", p=P)

            w1_free = [(VE, ve_x1), (VE, ve_x1)]
            w3_free = [(VE, ve_x1), (VE, ve_x1)]
            w2_free = [(VE, ve_x1), (VE, ve_x1)]
            w_ready = {}
            w_emitted = set()

            def load_w(ex, kind, idx):
                if ex >= NEXP or (ex, kind, idx) in w_emitted:
                    return
                w_emitted.add((ex, kind, idx))
                slot = idx % 2
                if kind == 1:
                    srcs = w1t_r[ex] if ex < NE else ws1t_r
                    srcap = srcs[:, :, idx * 256:(idx + 1) * 256]
                    dst = W1T[:, :, slot, :]
                    sem, gate = LDw1[slot], tuple(w1_free[slot])
                elif kind == 3:
                    srcs = w3t_r[ex] if ex < NE else ws3t_r
                    srcap = srcs[:, :, idx * 256:(idx + 1) * 256]
                    dst = W3T[:, :, slot, :]
                    sem, gate = LDw3[slot], tuple(w3_free[slot])
                else:
                    srcs = w2t_r[ex] if ex < NE else ws2t_r
                    srcap = srcs[:, :, idx * 512:(idx + 1) * 512]
                    dst = W2T[:, :, slot, :]
                    sem, gate = LDw2[slot], tuple(w2_free[slot])

                @block.gpsimd
                def _(g, srcap=srcap, dst=dst, sem=sem, gate=gate):
                    sm, val = gate
                    if sm is not None:
                        g.wait_ge(sm.h, val)
                    sem.inc(g.dma_start(out=dst, in_=srcap), 16)
                w_ready[(ex, kind, idx)] = (sem, sem.n)

            arena_free_z = [(VE, v_sb_all), (VE, v_stb)]
            last_sil = [0]
            last_hmul = [0]
            slotfree_r = [(None, 0), (None, 0)]
            h_readers_done = [0]

            for ex in range(NEXP):
                is_shared = (ex == NEXP - 1)
                for q in range(2):
                    load_w(ex, 1, q)
                    load_w(ex, 3, q)
                for kg in range(2):
                    load_w(ex, 2, kg)

                for ub in range(UB):
                    par = ub % 2
                    ps = pAR[par]

                    q = ub // 2
                    uo = (ub % 2) * P

                    @block.tensor
                    def _(t, ex=ex, ub=ub, q=q, uo=uo, ps=ps,
                          gate=tuple(arena_free_z[par])):
                        r1 = w_ready[(ex, 1, q)]
                        r3 = w_ready[(ex, 3, q)]
                        t.wait_ge(r1[0].h, r1[1])
                        t.wait_ge(r3[0].h, r3[1])
                        sm, val = gate
                        if sm is not None:
                            t.wait_ge(sm.h, val)
                        for m in range(DC):
                            l1 = W1T[:, m, q % 2, uo:uo + P]
                            l3 = W3T[:, m, q % 2, uo:uo + P]
                            t.matmul(ps[:, 0:512], l1, flat2(xfT[:, m, 0:4, :]),
                                     start=(m == 0), stop=(m == DC - 1),
                                     skip_group_check=True)
                            t.matmul(ps[:, 512:1024], l1, flat2(xfT[:, m, 4:8, :]),
                                     start=(m == 0), stop=(m == DC - 1),
                                     skip_group_check=True)
                            t.matmul(ps[:, 1024:1536], l3, flat2(xfT[:, m, 0:4, :]),
                                     start=(m == 0), stop=(m == DC - 1),
                                     skip_group_check=True)
                            ins = t.matmul(ps[:, 1536:2048], l3, flat2(xfT[:, m, 4:8, :]),
                                           start=(m == 0), stop=(m == DC - 1),
                                           skip_group_check=True)
                        PE.inc(ins)
                    pe_z = PE.n
                    if ub % 2 == 1:
                        w1_free[q % 2] = (PE, pe_z)
                        w3_free[q % 2] = (PE, pe_z)
                        if q + 2 <= 3:
                            load_w(ex, 1, q + 2)
                            load_w(ex, 3, q + 2)
                        else:
                            load_w(ex + 1, 1, q - 2)
                            load_w(ex + 1, 3, q - 2)

                    for th in range(2):
                        z1 = ps[:, th * 512:(th + 1) * 512]
                        z3 = ps[:, 1024 + th * 512:1024 + (th + 1) * 512]
                        h_sl = hbuf[:, ub, th * 512:(th + 1) * 512]
                        if not is_shared:
                            sbs = Sb[:, ex, th * 512:(th + 1) * 512]

                            @block.vector
                            def _(v, pe_z=pe_z, z1=z1, sbs=sbs, ls=last_sil[0]):
                                v.wait_ge(PE.h, pe_z)
                                if ex == 0 and ub == 0 and th == 0:
                                    v.wait_ge(VE.h, v_sb_all)
                                if ls:
                                    v.wait_ge(SE.h, ls)
                                VE.inc(v.tensor_tensor(tmp1[:], z1, sbs, OP.mult))
                            v_t1 = VE.n

                            @block.scalar
                            def _(s, v_t1=v_t1):
                                s.wait_ge(VE.h, v_t1)
                                SE.inc(s.activation(out=silb[:], in_=tmp1[:], func=AF.Silu))
                            s_sil = SE.n
                            last_sil[0] = s_sil

                            @block.vector
                            def _(v, z3=z3, sbs=sbs, h_sl=h_sl, s_sil=s_sil,
                                  hrd=h_readers_done[0], lh=last_hmul[0]):
                                if lh:
                                    v.wait_ge(VE.h, lh)
                                v_t3 = VE.inc(v.tensor_tensor(tmp3[:], z3, sbs, OP.mult))
                                v.wait_ge(SE.h, s_sil)
                                v.wait_ge(PE.h, hrd)
                                v.wait_ge(VE.h, v_t3)
                                VE.inc(v.tensor_tensor(h_sl, silb[:], tmp3[:], OP.mult))
                        else:
                            @block.scalar
                            def _(s, pe_z=pe_z, z1=z1, lh=last_hmul[0]):
                                s.wait_ge(PE.h, pe_z)
                                if lh:
                                    s.wait_ge(VE.h, lh)
                                SE.inc(s.activation(out=silb[:], in_=z1, func=AF.Silu))
                            s_sil = SE.n
                            last_sil[0] = s_sil

                            @block.vector
                            def _(v, z3=z3, h_sl=h_sl, s_sil=s_sil,
                                  hrd=h_readers_done[0]):
                                v.wait_ge(SE.h, s_sil)
                                v.wait_ge(PE.h, hrd)
                                VE.inc(v.tensor_tensor(h_sl, silb[:], z3, OP.mult))
                        last_hmul[0] = VE.n
                    arena_free_z[par] = (VE, VE.n)
                ve_h_all = VE.n

                for kg in range(4):
                    for tb in range(TBM):
                        par = tb % 2
                        slot = pAR[par][:, 0:512]

                        @block.tensor
                        def _(t, ex=ex, kg=kg, tb=tb, slot=slot,
                              gate=tuple(slotfree_r[par]),
                              zgate=tuple(arena_free_z[par]), ve_h_all=ve_h_all):
                            r2 = w_ready[(ex, 2, kg)]
                            t.wait_ge(r2[0].h, r2[1])
                            t.wait_ge(VE.h, ve_h_all)
                            sm, val = gate
                            if sm is not None:
                                t.wait_ge(sm.h, val)
                            if kg == 0 and tb < 2:
                                sm2, val2 = zgate
                                if sm2 is not None:
                                    t.wait_ge(sm2.h, val2)
                            for uc in range(UB):
                                ins = t.matmul(slot, hbuf[:, uc, tb * P:(tb + 1) * P],
                                               W2T[:, uc, kg % 2, :],
                                               start=(uc == 0), stop=(uc == UB - 1))
                            PE.inc(ins)
                        pe_r = PE.n
                        if tb == TBM - 1:
                            w2_free[kg % 2] = (PE, pe_r)
                            if kg + 2 <= 3:
                                load_w(ex, 2, kg + 2)
                            else:
                                load_w(ex + 1, 2, kg - 2)

                        @block.vector
                        def _(v, kg=kg, tb=tb, slot=slot, pe_r=pe_r,
                              aw=accw[(kg, tb)]):
                            v.wait_ge(PE.h, pe_r)
                            v.wait_ge(VE.h, aw)
                            a_sl = acc[:, tb, kg * 512:(kg + 1) * 512]
                            VE.inc(v.tensor_tensor(a_sl, a_sl, slot, OP.add))
                        slotfree_r[par] = (VE, VE.n)
                        accw[(kg, tb)] = VE.n
                h_readers_done[0] = PE.n
                arena_free_z = [(VE, VE.n), (VE, VE.n)]
            ve_final = VE.n

            @block.gpsimd
            def _(g):
                g.wait_ge(VE.h, ve_final)
                LDo.inc(g.dma_start(out=out_t[:], in_=acc[:]), 16)
                g.wait_ge(LDo.h, LDo.n)

    return nc


def _host_inputs(L0_attn_x, L0_ffn_gate_w, L0_ffn_w1, L0_ffn_w2, L0_ffn_w3,
                 L0_ffn_ws1, L0_ffn_ws2, L0_ffn_ws3):
    xf = np.ascontiguousarray(np.asarray(L0_attn_x, dtype=np.float32))
    gw = np.ascontiguousarray(np.asarray(L0_ffn_gate_w, dtype=np.float32))
    w1 = np.ascontiguousarray(np.asarray(L0_ffn_w1, dtype=np.float32))
    w2 = np.ascontiguousarray(np.asarray(L0_ffn_w2, dtype=np.float32))
    w3 = np.ascontiguousarray(np.asarray(L0_ffn_w3, dtype=np.float32))
    ws1 = np.ascontiguousarray(np.asarray(L0_ffn_ws1, dtype=np.float32))
    ws2 = np.ascontiguousarray(np.asarray(L0_ffn_ws2, dtype=np.float32))
    ws3 = np.ascontiguousarray(np.asarray(L0_ffn_ws3, dtype=np.float32))
    gwT = np.ascontiguousarray(gw.T)
    w1t = np.ascontiguousarray(w1.transpose(0, 2, 1))
    w2t = np.ascontiguousarray(w2.transpose(0, 2, 1))
    w3t = np.ascontiguousarray(w3.transpose(0, 2, 1))
    ws1t = np.ascontiguousarray(ws1.T)
    ws2t = np.ascontiguousarray(ws2.T)
    ws3t = np.ascontiguousarray(ws3.T)
    in_maps = []
    for c in range(8):
        b, h = c // 2, c % 2
        mine = xf[b, h * 1024:(h + 1) * 1024]
        other = xf[b, (1 - h) * 1024:(2 - h) * 1024]
        xc = np.ascontiguousarray(np.concatenate([mine, other], axis=0))
        in_maps.append(dict(x=xc, gwT=gwT, w1t=w1t, w2t=w2t, w3t=w3t,
                            ws1t=ws1t, ws2t=ws2t, ws3t=ws3t))
    return in_maps


def kernel(L0_attn_x, L0_ffn_gate_w, L0_ffn_w1, L0_ffn_w2, L0_ffn_w3,
           L0_ffn_ws1, L0_ffn_ws2, L0_ffn_ws3):
    in_maps = _host_inputs(L0_attn_x, L0_ffn_gate_w, L0_ffn_w1, L0_ffn_w2,
                           L0_ffn_w3, L0_ffn_ws1, L0_ffn_ws2, L0_ffn_ws3)
    nc = build()
    res = run_bass_kernel_spmd(nc, in_maps, core_ids=list(range(8)))
    out = np.empty((4, 2048, 2048), dtype=np.float32)
    for c in range(8):
        b, h = c // 2, c % 2
        out[b, h * 1024:(h + 1) * 1024] = res.results[c]["out"]
    return out


# version 12
# speedup vs baseline: 1.7750x; 1.0375x over previous
import numpy as np
from contextlib import ExitStack

import concourse.bass as bass
import concourse.mybir as mybir
from concourse.bass_utils import run_bass_kernel_spmd
from concourse.alu_op_type import AluOpType as OP

F32 = mybir.dt.float32
BF16 = mybir.dt.bfloat16
U32 = mybir.dt.uint32
AF = mybir.ActivationFunctionType
AX = mybir.AxisListType

P = 128
D = 2048
SEQ = 2048
TB = 16
TBM = 8
DC = 16
H = 1024
UB = 8
NE = 8
NEXP = 9
EPS = 1e-5
INV_D = 1.0 / D

NATW = 2
W1WIN = 4
W2WIN = 4


class Sem:
    def __init__(self, h):
        self.h = h
        self.n = 0

    def inc(self, instr, k=1):
        instr.then_inc(self.h, k)
        self.n += k
        return self.n


def flat2(ap):
    return ap.rearrange("p a b -> p (a b)")


def build(want_taps=(), stage="full"):
    nc = bass.Bass("TRN2", target_bir_lowering=False, debug=False)

    x = nc.dram_tensor("x", [SEQ, D], F32, kind="ExternalInput").ap()
    gwT = nc.dram_tensor("gwT", [D, NE], F32, kind="ExternalInput").ap()
    w1t = nc.dram_tensor("w1t", [NE, D, H], F32, kind="ExternalInput").ap()
    w2t = nc.dram_tensor("w2t", [NE, H, D], F32, kind="ExternalInput").ap()
    w3t = nc.dram_tensor("w3t", [NE, D, H], F32, kind="ExternalInput").ap()
    ws1t = nc.dram_tensor("ws1t", [D, H], F32, kind="ExternalInput").ap()
    ws2t = nc.dram_tensor("ws2t", [H, D], F32, kind="ExternalInput").ap()
    ws3t = nc.dram_tensor("ws3t", [D, H], F32, kind="ExternalInput").ap()
    out = nc.dram_tensor("out", [TBM * P, D], F32, kind="ExternalOutput").ap()

    x_t = x.rearrange("(tb p) d -> p tb d", p=P)
    out_t = out.rearrange("(tb p) d -> p tb d", p=P)

    with ExitStack() as top:
        e = top.enter_context
        acc = e(nc.sbuf_tensor("acc", [P, TBM, D], F32))
        identt = e(nc.sbuf_tensor("identt", [P, P], F32))
        selt = e(nc.sbuf_tensor("selt", [NE, NE * P], BF16))
        iota8 = e(nc.sbuf_tensor("iota8", [P, NE], F32))
        gT = e(nc.sbuf_tensor("gT", [P, DC, NE], BF16))
        gws = e(nc.sbuf_tensor("gws", [P, DC, NE], F32))
        st_sum = e(nc.sbuf_tensor("st_sum", [P, TB], F32))
        st_ssq = e(nc.sbuf_tensor("st_ssq", [P, TB], F32))
        st_mean = e(nc.sbuf_tensor("st_mean", [P, TB], F32))
        st_var = e(nc.sbuf_tensor("st_var", [P, TB], F32))
        st_std = e(nc.sbuf_tensor("st_std", [P, TB], F32))
        st_rstd = e(nc.sbuf_tensor("st_rstd", [P, TB], F32))
        st_nmr = e(nc.sbuf_tensor("st_nmr", [P, TB], F32))
        st_tmp = e(nc.sbuf_tensor("st_tmp", [P, TB], F32))
        sqscr = e(nc.sbuf_tensor("sqscr", [P, D], F32))
        sm_mxn = e(nc.sbuf_tensor("sm_mxn", [P, 2], F32))
        sm_rs = e(nc.sbuf_tensor("sm_rs", [P, 2], F32))
        sm_lnr = e(nc.sbuf_tensor("sm_lnr", [P, 2], F32))
        sm_b2 = e(nc.sbuf_tensor("sm_b2", [P, 2], F32))

        pA = e(nc.psum_tensor("pA", [P, 512 * 4], F32))
        pB = e(nc.psum_tensor("pB", [P, 512 * 4], F32))
        pAR = [pA, pB]

        LDa = Sem(e(nc.semaphore("LDa")))
        LDx = Sem(e(nc.semaphore("LDx")))
        LDg = Sem(e(nc.semaphore("LDg")))
        LDw1 = [Sem(e(nc.semaphore(f"LDw1_{i}"))) for i in range(2)]
        LDw3 = [Sem(e(nc.semaphore(f"LDw3_{i}"))) for i in range(2)]
        LDw2 = [Sem(e(nc.semaphore(f"LDw2_{i}"))) for i in range(2)]
        LDo = Sem(e(nc.semaphore("LDo")))
        GP = Sem(e(nc.semaphore("GP")))
        TP = Sem(e(nc.semaphore("TP")))
        PE = Sem(e(nc.semaphore("PE")))
        VE = Sem(e(nc.semaphore("VE")))
        SE = Sem(e(nc.semaphore("SE")))

        block = e(nc.Block())

        def tap(name, ap, waits=()):
            if name not in want_taps:
                return
            t = nc.dram_tensor("tap_" + name, list(ap.shape), ap.dtype,
                               kind="ExternalOutput").ap()

            @block.gpsimd
            def _(g, ap=ap, t=t, waits=tuple(waits)):
                for s, v in waits:
                    g.wait_ge(s.h, v)
                LDo.inc(g.dma_start(out=t, in_=ap), 16)

        @block.gpsimd
        def _(g):
            GP.inc(g.iota(iota8[:], pattern=[[1, NE]], base=0, channel_multiplier=0,
                          allow_small_or_imprecise_dtypes=True), 1)
            GP.inc(g.iota(identt[:], pattern=[[0, P]], base=0, channel_multiplier=1,
                          allow_small_or_imprecise_dtypes=True), 1)
            GP.inc(g.iota(sqscr[:, 0:P], pattern=[[1, P]], base=0, channel_multiplier=0,
                          allow_small_or_imprecise_dtypes=True), 1)
            GP.inc(g.iota(selt[:], pattern=[[0, NE * P]], base=0, channel_multiplier=1,
                          allow_small_or_imprecise_dtypes=True), 1)
            GP.inc(g.iota(sqscr[0:NE, P:P + NE * P].rearrange("k (e j) -> k e j", j=P),
                          pattern=[[1, NE], [0, P]], base=0, channel_multiplier=0,
                          allow_small_or_imprecise_dtypes=True), 1)
            LDg.inc(g.dma_start(out=gws[:], in_=gwT.rearrange("(m p) e -> p m e", p=P)), 16)
            LDa.inc(g.dma_start(out=acc[:], in_=x_t[:, 0:TBM, :]), 16)
        gp_consts = GP.n
        ld_gws = (LDg, LDg.n)
        ld_acc = (LDa, LDa.n)

        @block.vector
        def _(v):
            v.wait_ge(GP.h, gp_consts)
            v.wait_ge(ld_gws[0].h, ld_gws[1])
            VE.inc(v.tensor_tensor(identt[:], identt[:], sqscr[:, 0:P], OP.is_equal))
            VE.inc(v.tensor_tensor(selt[:], selt[:], sqscr[0:NE, P:P + NE * P], OP.is_equal))
            VE.inc(v.tensor_copy(gT[:], gws[:]))
        ve_consts = VE.n

        st_m2 = e(nc.sbuf_tensor("st_m2", [P, TB], F32))

        def emit_ln_group(items, ld_waits=None, scalar_pre=None, vector_pre=None):
            cols = [c for c, _, _ in items]
            lo, hi = min(cols), max(cols) + 1
            csl = slice(lo, hi)

            @block.vector
            def _(v):
                if vector_pre is not None:
                    v.wait_ge(vector_pre[0].h, vector_pre[1])
                for i, (c, src_ap, _) in enumerate(items):
                    if ld_waits is not None and ld_waits[i] is not None:
                        v.wait_ge(ld_waits[i][0].h, ld_waits[i][1])
                    VE.inc(v.tensor_reduce(st_sum[:, c:c + 1], src_ap, AX.X, OP.add,
                                           negate=True))
            v_red = VE.n

            @block.scalar
            def _(s):
                if scalar_pre is not None:
                    s.wait_ge(scalar_pre[0].h, scalar_pre[1])
                for i, (c, src_ap, _) in enumerate(items):
                    if ld_waits is not None and ld_waits[i] is not None:
                        s.wait_ge(ld_waits[i][0].h, ld_waits[i][1])
                    SE.inc(s.activation(out=sqscr[:], in_=src_ap, func=AF.Square,
                                        accum_out=st_ssq[:, c:c + 1]))
            s_sq = SE.n

            @block.vector
            def _(v):
                v.wait_ge(VE.h, v_red)
                v.wait_ge(SE.h, s_sq)
                VE.inc(v.tensor_scalar_mul(st_mean[:, csl], st_sum[:, csl], INV_D))
                VE.inc(v.tensor_scalar_mul(st_tmp[:, csl], st_ssq[:, csl], INV_D))
            v_d1 = VE.n

            @block.vector
            def _(v):
                v.wait_ge(VE.h, v_d1)
                VE.inc(v.tensor_tensor(st_m2[:, csl], st_mean[:, csl], st_mean[:, csl],
                                       OP.mult))
            v_d2 = VE.n

            @block.vector
            def _(v):
                v.wait_ge(VE.h, v_d2)
                VE.inc(v.tensor_tensor(st_var[:, csl], st_tmp[:, csl], st_m2[:, csl],
                                       OP.subtract))
            v_d3 = VE.n

            @block.vector
            def _(v):
                v.wait_ge(VE.h, v_d3)
                VE.inc(v.tensor_scalar_add(st_var[:, csl], st_var[:, csl], EPS))
            v_d4 = VE.n

            @block.scalar
            def _(s):
                s.wait_ge(VE.h, v_d4)
                SE.inc(s.activation(out=st_std[:, csl], in_=st_var[:, csl], func=AF.Sqrt))
            s_std = SE.n

            @block.vector
            def _(v):
                v.wait_ge(SE.h, s_std)
                VE.inc(v.reciprocal(st_rstd[:, csl], st_std[:, csl]))
            v_rec = VE.n

            @block.vector
            def _(v):
                v.wait_ge(VE.h, v_rec)
                VE.inc(v.tensor_tensor(st_nmr[:, csl], st_mean[:, csl], st_rstd[:, csl],
                                       OP.mult))
            v_nmr = VE.n

            outs = []
            for c, src_ap, dst_ap in items:
                @block.scalar
                def _(s, c=c, src_ap=src_ap, dst_ap=dst_ap):
                    s.wait_ge(VE.h, v_nmr)
                    SE.inc(s.activation(out=dst_ap, in_=src_ap, func=AF.Identity,
                                        bias=st_nmr[:, c:c + 1],
                                        scale=st_rstd[:, c:c + 1]))
                outs.append(SE.n)
            return outs

        def emit_ln(src_ap, dst_ap, col, ld_wait=None, scalar_pre=None, vector_pre=None):
            return emit_ln_group([(col, src_ap, dst_ap)], ld_waits=[ld_wait],
                                 scalar_pre=scalar_pre, vector_pre=vector_pre)[0]

        arena_free = [(None, 0), (None, 0)]
        with ExitStack() as attn_scope:
            ea = attn_scope.enter_context
            Xn = ea(nc.sbuf_tensor("Xn", [P, TB, D], BF16))
            XnT = ea(nc.sbuf_tensor("XnT", [P, DC, TBM, P], BF16))
            xring = [ea(nc.sbuf_tensor("xring0", [P, D], F32))]
            Asb = [ea(nc.sbuf_tensor("Asb0", [P, D], BF16))]
            ATsl = [ea(nc.sbuf_tensor("ATsl0", [P, DC, 4, P], BF16))]

            s_norm = emit_ln_group(
                [(tb, acc[:, tb, :], Xn[:, tb, :]) for tb in range(TBM)],
                ld_waits=[ld_acc] * TBM,
                scalar_pre=(VE, ve_consts), vector_pre=None)
            ring_free = [(None, 0)]
            for tb in range(TBM, TB):
                @block.gpsimd
                def _(g, tb=tb, gate=tuple(ring_free[0])):
                    sm, val = gate
                    if sm is not None:
                        g.wait_ge(sm.h, val)
                    LDx.inc(g.dma_start(out=xring[0][:], in_=x_t[:, tb, :]), 16)
                sn = emit_ln(xring[0][:], Xn[:, tb, :], tb, ld_wait=(LDx, LDx.n))
                s_norm.append(sn)
                ring_free[0] = (SE, sn)
            s_ln1_all = SE.n
            tap("ln1", Xn[:], waits=[(SE, s_ln1_all)])

            if stage == "ln1":
                @block.gpsimd
                def _(g):
                    g.wait_ge(LDo.h, LDo.n)
                return nc

            for tb in range(TBM):
                @block.sync
                def _(sy, tb=tb):
                    sy.wait_ge(SE.h, s_norm[tb])
                    TP.inc(sy.dma_start(out=XnT[:, :, tb, :], in_=Xn[:, tb, :],
                                        transpose=True), 16)
                    sy.wait_ge(TP.h, TP.n)
            tp_xnt = TP.n
            tap("xnt", XnT[:], waits=[(TP, tp_xnt)])

            if stage == "xnt":
                @block.gpsimd
                def _(g):
                    g.wait_ge(LDo.h, LDo.n)
                return nc

            accw = {}
            asb_free = [(None, 0)]
            atsl_free = [(None, 0)]
            tp_at = {}
            for g_r in range(4):
                for ibl in range(4):
                    gi = 4 * g_r + ibl
                    par = gi % 2
                    ps = pAR[par]

                    @block.tensor
                    def _(t, gi=gi, ps=ps, gate=tuple(arena_free[par])):
                        if gi == 2:
                            t.wait_ge(SE.h, s_ln1_all)
                        sm, val = gate
                        if sm is not None:
                            t.wait_ge(sm.h, val)
                        for ct in range(TB):
                            if gi < 2:
                                t.wait_ge(SE.h, s_norm[ct])
                            for jr in range(4):
                                ins = t.matmul(ps[:, jr * 512:(jr + 1) * 512],
                                               Xn[:, ct, gi * P:(gi + 1) * P],
                                               Xn[:, ct, jr * 512:(jr + 1) * 512],
                                               start=(ct == 0), stop=(ct == TB - 1),
                                               skip_group_check=True)
                        PE.inc(ins)
                    pe_gram = PE.n

                    if stage == "gram0" and gi == 0:
                        @block.vector
                        def _(v, ps=ps, pe_gram=pe_gram):
                            v.wait_ge(PE.h, pe_gram)
                            VE.inc(v.tensor_copy(sqscr[:], ps[:]))
                        tap("G0", sqscr[:], waits=[(VE, VE.n)])

                        @block.gpsimd
                        def _(g):
                            g.wait_ge(LDo.h, LDo.n)
                        return nc

                    mxn = sm_mxn[:, par:par + 1]
                    rs = sm_rs[:, par:par + 1]
                    lnr = sm_lnr[:, par:par + 1]
                    b2 = sm_b2[:, par:par + 1]

                    @block.vector
                    def _(v, ps=ps, pe_gram=pe_gram, mxn=mxn):
                        v.wait_ge(PE.h, pe_gram)
                        VE.inc(v.tensor_reduce(mxn, ps[:], AX.X, OP.max, negate=True))
                    v_mxn = VE.n

                    @block.scalar
                    def _(s, gi=gi, ps=ps, v_mxn=v_mxn, mxn=mxn, rs=rs,
                          gate=tuple(asb_free[0])):
                        s.wait_ge(VE.h, v_mxn)
                        sm, val = gate
                        if sm is not None:
                            s.wait_ge(sm.h, val)
                        SE.inc(s.activation(out=Asb[0][:], in_=ps[:], func=AF.Exp,
                                            bias=mxn, accum_out=rs))
                    s_e1 = SE.n

                    @block.scalar
                    def _(s, s_e1=s_e1, rs=rs, lnr=lnr):
                        s.wait_ge(SE.h, s_e1)
                        SE.inc(s.activation(out=lnr, in_=rs, func=AF.Ln))
                    s_ln = SE.n

                    @block.vector
                    def _(v, s_ln=s_ln, mxn=mxn, lnr=lnr, b2=b2):
                        v.wait_ge(SE.h, s_ln)
                        VE.inc(v.tensor_tensor(b2, mxn, lnr, OP.subtract))
                    v_b2 = VE.n

                    @block.scalar
                    def _(s, gi=gi, ps=ps, v_b2=v_b2, b2=b2):
                        s.wait_ge(VE.h, v_b2)
                        SE.inc(s.activation(out=Asb[0][:], in_=ps[:], func=AF.Exp,
                                            bias=b2))
                    s_e2 = SE.n
                    arena_free[par] = (SE, s_e2)

                    if stage == "gram1" and gi == 0:
                        tap("A0", Asb[0][:], waits=[(SE, s_e2)])

                        @block.gpsimd
                        def _(g):
                            g.wait_ge(LDo.h, LDo.n)
                        return nc

                    @block.sync
                    def _(sy, g_r=g_r, ibl=ibl, gi=gi, s_e2=s_e2,
                          gate=tuple(atsl_free[0])):
                        sy.wait_ge(SE.h, s_e2)
                        sm, val = gate
                        if sm is not None and ibl == 0:
                            sy.wait_ge(sm.h, val)
                        TP.inc(sy.dma_start(out=ATsl[0][:, :, ibl, :],
                                            in_=Asb[0][:], transpose=True), 16)
                        sy.wait_ge(TP.h, TP.n)
                    asb_free[0] = (TP, TP.n)

                    if stage == "at1" and gi == 0:
                        tap("AT0", ATsl[0][:], waits=[(TP, TP.n)])

                        @block.gpsimd
                        def _(g):
                            g.wait_ge(LDo.h, LDo.n)
                        return nc
                tp_at[g_r] = TP.n

                for tb in range(TBM):
                    par = tb % 2
                    slot = pAR[par][:, 0:512]

                    @block.tensor
                    def _(t, g_r=g_r, tb=tb, slot=slot, gate=tuple(arena_free[par])):
                        t.wait_ge(TP.h, tp_at[g_r])
                        if g_r == 0 and tb == 0:
                            t.wait_ge(TP.h, tp_xnt)
                        sm, val = gate
                        if sm is not None:
                            t.wait_ge(sm.h, val)
                        for m in range(DC):
                            ins = t.matmul(slot, XnT[:, m, tb, :],
                                           flat2(ATsl[0][:, m, :, :]),
                                           start=(m == 0), stop=(m == DC - 1))
                        PE.inc(ins)
                    pe_out = PE.n

                    @block.vector
                    def _(v, g_r=g_r, tb=tb, slot=slot, pe_out=pe_out):
                        v.wait_ge(PE.h, pe_out)
                        a_sl = acc[:, tb, g_r * 512:(g_r + 1) * 512]
                        VE.inc(v.tensor_tensor(a_sl, a_sl, slot, OP.add))
                    arena_free[par] = (VE, VE.n)
                    accw[(g_r, tb)] = VE.n
                atsl_free[0] = (PE, PE.n)
            ve_x1 = VE.n
            tap("x1", acc[:], waits=[(VE, ve_x1)])


        if stage == "attn":
            @block.gpsimd
            def _(g):
                g.wait_ge(LDo.h, LDo.n)
            return nc
        with ExitStack() as ffn_scope:
            ef = ffn_scope.enter_context
            xfT = ef(nc.sbuf_tensor("xfT", [P, DC, TBM, P], BF16))
            xfring = ef(nc.sbuf_tensor("xfring", [P, 2, D], BF16))
            scores = ef(nc.sbuf_tensor("scores", [P, TBM, NE], F32))
            topv = ef(nc.sbuf_tensor("topv", [P, TBM, NE], F32))
            topi = ef(nc.sbuf_tensor("topi", [P, TBM, NE], U32))
            topf = ef(nc.sbuf_tensor("topf", [P, TBM, NE], F32))
            Ssc = ef(nc.sbuf_tensor("Ssc", [P, TBM, NE], F32))
            STb = ef(nc.sbuf_tensor("STb", [NE, TBM * P], BF16))
            Sb = ef(nc.sbuf_tensor("Sb", [P, NE, H], BF16))
            W1T = ef(nc.sbuf_tensor("W1T", [P, DC, 2, 256], BF16))
            W3T = ef(nc.sbuf_tensor("W3T", [P, DC, 2, 256], BF16))
            W2T = ef(nc.sbuf_tensor("W2T", [P, UB, 2, 512], BF16))
            hbuf = ef(nc.sbuf_tensor("hbuf", [P, UB, H], BF16))
            tmp1 = ef(nc.sbuf_tensor("tmp1", [P, 512], F32))
            tmp3 = ef(nc.sbuf_tensor("tmp3", [P, 512], F32))
            silb = ef(nc.sbuf_tensor("silb", [P, 512], F32))

            tp_xf = []
            gate_prev = (None, 0)
            for grp in range(4):
                tbs = range(grp * 2, grp * 2 + 2)
                sns = emit_ln_group(
                    [(tb, acc[:, tb, :], xfring[:, tb % 2, :]) for tb in tbs],
                    scalar_pre=(VE, ve_x1) if grp == 0 else gate_prev,
                    vector_pre=(VE, ve_x1) if grp == 0 else None)
                for j, tb in enumerate(tbs):
                    @block.sync
                    def _(sy, tb=tb, sn=sns[j]):
                        sy.wait_ge(SE.h, sn)
                        TP.inc(sy.dma_start(out=xfT[:, :, tb, :],
                                            in_=xfring[:, tb % 2, :],
                                            transpose=True), 16)
                        sy.wait_ge(TP.h, TP.n)
                    tp_xf.append(TP.n)
                gate_prev = (TP, TP.n)
            tp_xfT = TP.n

            gate_arena0 = tuple(arena_free[0])
            gate_arena1 = tuple(arena_free[1])
            pe_gate = []
            s_gate_evac = []
            for tb in range(TBM):
                @block.tensor
                def _(t, tb=tb):
                    t.wait_ge(VE.h, ve_consts)
                    t.wait_ge(TP.h, tp_xf[tb])
                    if tb < 2:
                        sm, val = gate_arena0
                        if sm is not None:
                            t.wait_ge(sm.h, val)
                    if tb >= 2:
                        t.wait_ge(SE.h, s_gate_evac[tb - 2])
                    for m in range(DC):
                        ins = t.matmul(pAR[0][:, (tb % 2) * 512:(tb % 2) * 512 + NE],
                                       xfT[:, m, tb, :], gT[:, m, 0:NE],
                                       start=(m == 0), stop=(m == DC - 1))
                    PE.inc(ins)
                pe_gate.append(PE.n)

                @block.scalar
                def _(s, tb=tb):
                    s.wait_ge(PE.h, pe_gate[tb])
                    SE.inc(s.activation(out=scores[:, tb, :],
                                        in_=pAR[0][:, (tb % 2) * 512:(tb % 2) * 512 + NE],
                                        func=AF.Sigmoid))
                s_gate_evac.append(SE.n)
            s_scores = SE.n
            tap("scores", scores[:], waits=[(SE, s_scores)])

            if stage == "scores":
                @block.gpsimd
                def _(g):
                    g.wait_ge(LDo.h, LDo.n)
                return nc

            @block.vector
            def _(v):
                v.wait_ge(SE.h, s_scores)
                for tb in range(TBM):
                    VE.inc(v.max(topv[:, tb, :], scores[:, tb, :]))
            v_max = VE.n

            @block.vector
            def _(v):
                v.wait_ge(VE.h, v_max)
                for tb in range(TBM):
                    VE.inc(v.max_index(topi[:, tb, :], topv[:, tb, :], scores[:, tb, :]))
            v_mi = VE.n

            @block.vector
            def _(v):
                v.wait_ge(VE.h, v_mi)
                v.wait_ge(GP.h, 1)
                for tb in range(TBM):
                    VE.inc(v.tensor_copy(topf[:, tb, :], topi[:, tb, :]))
            v_tf = VE.n

            e3t = ef(nc.sbuf_tensor("e3t", [P, TBM, NE], F32))

            @block.vector
            def _(v):
                v.wait_ge(VE.h, v_tf)
                for tb in range(TBM):
                    VE.inc(v.tensor_scalar(Ssc[:, tb, :], iota8[:], topf[:, tb, 0:1],
                                           None, OP.is_equal))
                    VE.inc(v.tensor_scalar(e3t[:, tb, :], iota8[:], topf[:, tb, 1:2],
                                           None, OP.is_equal))
            v_eq = VE.n

            @block.vector
            def _(v):
                v.wait_ge(VE.h, v_eq)
                for tb in range(TBM):
                    VE.inc(v.tensor_scalar(Ssc[:, tb, :], Ssc[:, tb, :], topv[:, tb, 0:1],
                                           None, OP.mult))
            v_s1 = VE.n

            @block.vector
            def _(v):
                v.wait_ge(VE.h, v_s1)
                for tb in range(TBM):
                    VE.inc(v.scalar_tensor_tensor(Ssc[:, tb, :], e3t[:, tb, :],
                                                  topv[:, tb, 1:2], Ssc[:, tb, :],
                                                  OP.mult, OP.add))
            v_S = VE.n
            tap("S", Ssc[:], waits=[(VE, v_S)])

            ve_st = []
            for tb in range(TBM):
                @block.tensor
                def _(t, tb=tb):
                    t.wait_ge(VE.h, v_S)
                    t.wait_ge(VE.h, ve_consts)
                    if tb == 0:
                        sm, val = gate_arena1
                        if sm is not None:
                            t.wait_ge(sm.h, val)
                    else:
                        t.wait_ge(VE.h, ve_st[tb - 1])
                    PE.inc(t.transpose(pAR[1][0:NE, 0:P], Ssc[:, tb, :], identt[:]))
                pe_st = PE.n

                @block.vector
                def _(v, tb=tb, pe_st=pe_st):
                    v.wait_ge(PE.h, pe_st)
                    VE.inc(v.tensor_copy(STb[0:NE, tb * P:(tb + 1) * P],
                                         pAR[1][0:NE, 0:P]))
                ve_st.append(VE.n)
            v_stb = VE.n

            ve_sb_evac = []
            for ex in range(NE):
                @block.tensor
                def _(t, ex=ex):
                    t.wait_ge(VE.h, v_stb)
                    if ex >= 1:
                        t.wait_ge(VE.h, ve_sb_evac[ex - 1])
                    t.matmul(pAR[0][:, 0:512], selt[:, ex * P:(ex + 1) * P],
                             STb[0:NE, 0:512], start=True, stop=True)
                    ins = t.matmul(pAR[0][:, 512:1024], selt[:, ex * P:(ex + 1) * P],
                                   STb[0:NE, 512:1024], start=True, stop=True)
                    PE.inc(ins)
                pe_sb = PE.n

                @block.vector
                def _(v, ex=ex, pe_sb=pe_sb):
                    v.wait_ge(PE.h, pe_sb)
                    VE.inc(v.tensor_copy(Sb[:, ex, :], pAR[0][:, 0:1024]))
                ve_sb_evac.append(VE.n)
            v_sb_all = VE.n
            tap("Sb", Sb[:], waits=[(VE, v_sb_all)])

            if stage == "sb":
                @block.gpsimd
                def _(g):
                    g.wait_ge(LDo.h, LDo.n)
                return nc

            w1t_r = w1t.rearrange("E (m p) u -> E p m u", p=P)
            w3t_r = w3t.rearrange("E (m p) u -> E p m u", p=P)
            w2t_r = w2t.rearrange("E (mu p) d -> E p mu d", p=P)
            ws1t_r = ws1t.rearrange("(m p) u -> p m u", p=P)
            ws3t_r = ws3t.rearrange("(m p) u -> p m u", p=P)
            ws2t_r = ws2t.rearrange("(mu p) d -> p mu d", p=P)

            w1_free = [(VE, ve_x1), (VE, ve_x1)]
            w3_free = [(VE, ve_x1), (VE, ve_x1)]
            w2_free = [(VE, ve_x1), (VE, ve_x1)]
            w_ready = {}
            w_emitted = set()

            def load_w(ex, kind, idx):
                if ex >= NEXP or (ex, kind, idx) in w_emitted:
                    return
                w_emitted.add((ex, kind, idx))
                slot = idx % 2
                if kind == 1:
                    srcs = w1t_r[ex] if ex < NE else ws1t_r
                    srcap = srcs[:, :, idx * 256:(idx + 1) * 256]
                    dst = W1T[:, :, slot, :]
                    sem, gate = LDw1[slot], tuple(w1_free[slot])
                elif kind == 3:
                    srcs = w3t_r[ex] if ex < NE else ws3t_r
                    srcap = srcs[:, :, idx * 256:(idx + 1) * 256]
                    dst = W3T[:, :, slot, :]
                    sem, gate = LDw3[slot], tuple(w3_free[slot])
                else:
                    srcs = w2t_r[ex] if ex < NE else ws2t_r
                    srcap = srcs[:, :, idx * 512:(idx + 1) * 512]
                    dst = W2T[:, :, slot, :]
                    sem, gate = LDw2[slot], tuple(w2_free[slot])

                @block.gpsimd
                def _(g, srcap=srcap, dst=dst, sem=sem, gate=gate):
                    sm, val = gate
                    if sm is not None:
                        g.wait_ge(sm.h, val)
                    sem.inc(g.dma_start(out=dst, in_=srcap), 16)
                w_ready[(ex, kind, idx)] = (sem, sem.n)

            arena_free_z = [(VE, v_sb_all), (VE, v_stb)]
            last_sil = [0]
            last_hmul = [0]
            slotfree_r = [(None, 0), (None, 0)]
            h_readers_done = [0]

            for ex in range(NEXP):
                is_shared = (ex == NEXP - 1)
                for q in range(2):
                    load_w(ex, 1, q)
                    load_w(ex, 3, q)
                for kg in range(2):
                    load_w(ex, 2, kg)

                for ub in range(UB):
                    par = ub % 2
                    ps = pAR[par]

                    q = ub // 2
                    uo = (ub % 2) * P

                    @block.tensor
                    def _(t, ex=ex, ub=ub, q=q, uo=uo, ps=ps,
                          gate=tuple(arena_free_z[par])):
                        r1 = w_ready[(ex, 1, q)]
                        r3 = w_ready[(ex, 3, q)]
                        t.wait_ge(r1[0].h, r1[1])
                        t.wait_ge(r3[0].h, r3[1])
                        sm, val = gate
                        if sm is not None:
                            t.wait_ge(sm.h, val)
                        for m in range(DC):
                            l1 = W1T[:, m, q % 2, uo:uo + P]
                            l3 = W3T[:, m, q % 2, uo:uo + P]
                            t.matmul(ps[:, 0:512], l1, flat2(xfT[:, m, 0:4, :]),
                                     start=(m == 0), stop=(m == DC - 1),
                                     skip_group_check=True)
                            t.matmul(ps[:, 512:1024], l1, flat2(xfT[:, m, 4:8, :]),
                                     start=(m == 0), stop=(m == DC - 1),
                                     skip_group_check=True)
                            t.matmul(ps[:, 1024:1536], l3, flat2(xfT[:, m, 0:4, :]),
                                     start=(m == 0), stop=(m == DC - 1),
                                     skip_group_check=True)
                            ins = t.matmul(ps[:, 1536:2048], l3, flat2(xfT[:, m, 4:8, :]),
                                           start=(m == 0), stop=(m == DC - 1),
                                           skip_group_check=True)
                        PE.inc(ins)
                    pe_z = PE.n
                    if ub % 2 == 1:
                        w1_free[q % 2] = (PE, pe_z)
                        w3_free[q % 2] = (PE, pe_z)
                        if q + 2 <= 3:
                            load_w(ex, 1, q + 2)
                            load_w(ex, 3, q + 2)
                        else:
                            load_w(ex + 1, 1, q - 2)
                            load_w(ex + 1, 3, q - 2)

                    for th in range(2):
                        z1 = ps[:, th * 512:(th + 1) * 512]
                        z3 = ps[:, 1024 + th * 512:1024 + (th + 1) * 512]
                        h_sl = hbuf[:, ub, th * 512:(th + 1) * 512]
                        if not is_shared:
                            sbs = Sb[:, ex, th * 512:(th + 1) * 512]

                            @block.vector
                            def _(v, pe_z=pe_z, z1=z1, sbs=sbs, ls=last_sil[0]):
                                v.wait_ge(PE.h, pe_z)
                                if ex == 0 and ub == 0 and th == 0:
                                    v.wait_ge(VE.h, v_sb_all)
                                if ls:
                                    v.wait_ge(SE.h, ls)
                                VE.inc(v.tensor_tensor(tmp1[:], z1, sbs, OP.mult))
                            v_t1 = VE.n

                            @block.scalar
                            def _(s, v_t1=v_t1):
                                s.wait_ge(VE.h, v_t1)
                                SE.inc(s.activation(out=silb[:], in_=tmp1[:], func=AF.Silu))
                            s_sil = SE.n
                            last_sil[0] = s_sil

                            @block.vector
                            def _(v, z3=z3, sbs=sbs, h_sl=h_sl, s_sil=s_sil,
                                  hrd=h_readers_done[0], lh=last_hmul[0]):
                                if lh:
                                    v.wait_ge(VE.h, lh)
                                v_t3 = VE.inc(v.tensor_tensor(tmp3[:], z3, sbs, OP.mult))
                                v.wait_ge(SE.h, s_sil)
                                v.wait_ge(PE.h, hrd)
                                v.wait_ge(VE.h, v_t3)
                                VE.inc(v.tensor_tensor(h_sl, silb[:], tmp3[:], OP.mult))
                        else:
                            @block.scalar
                            def _(s, pe_z=pe_z, z1=z1, lh=last_hmul[0]):
                                s.wait_ge(PE.h, pe_z)
                                if lh:
                                    s.wait_ge(VE.h, lh)
                                SE.inc(s.activation(out=silb[:], in_=z1, func=AF.Silu))
                            s_sil = SE.n
                            last_sil[0] = s_sil

                            @block.vector
                            def _(v, z3=z3, h_sl=h_sl, s_sil=s_sil,
                                  hrd=h_readers_done[0]):
                                v.wait_ge(SE.h, s_sil)
                                v.wait_ge(PE.h, hrd)
                                VE.inc(v.tensor_tensor(h_sl, silb[:], z3, OP.mult))
                        last_hmul[0] = VE.n
                    arena_free_z[par] = (VE, VE.n)
                ve_h_all = VE.n

                for kg in range(4):
                    for tb in range(TBM):
                        par = tb % 2
                        slot = pAR[par][:, 0:512]

                        @block.tensor
                        def _(t, ex=ex, kg=kg, tb=tb, slot=slot,
                              gate=tuple(slotfree_r[par]),
                              zgate=tuple(arena_free_z[par]), ve_h_all=ve_h_all):
                            r2 = w_ready[(ex, 2, kg)]
                            t.wait_ge(r2[0].h, r2[1])
                            t.wait_ge(VE.h, ve_h_all)
                            sm, val = gate
                            if sm is not None:
                                t.wait_ge(sm.h, val)
                            if kg == 0 and tb < 2:
                                sm2, val2 = zgate
                                if sm2 is not None:
                                    t.wait_ge(sm2.h, val2)
                            for uc in range(UB):
                                ins = t.matmul(slot, hbuf[:, uc, tb * P:(tb + 1) * P],
                                               W2T[:, uc, kg % 2, :],
                                               start=(uc == 0), stop=(uc == UB - 1))
                            PE.inc(ins)
                        pe_r = PE.n
                        if tb == TBM - 1:
                            w2_free[kg % 2] = (PE, pe_r)
                            if kg + 2 <= 3:
                                load_w(ex, 2, kg + 2)
                            else:
                                load_w(ex + 1, 2, kg - 2)

                        @block.vector
                        def _(v, kg=kg, tb=tb, slot=slot, pe_r=pe_r,
                              aw=accw[(kg, tb)]):
                            v.wait_ge(PE.h, pe_r)
                            v.wait_ge(VE.h, aw)
                            a_sl = acc[:, tb, kg * 512:(kg + 1) * 512]
                            VE.inc(v.tensor_tensor(a_sl, a_sl, slot, OP.add))
                        slotfree_r[par] = (VE, VE.n)
                        accw[(kg, tb)] = VE.n
                h_readers_done[0] = PE.n
                arena_free_z = [(VE, VE.n), (VE, VE.n)]
            ve_final = VE.n

            @block.gpsimd
            def _(g):
                g.wait_ge(VE.h, ve_final)
                LDo.inc(g.dma_start(out=out_t[:], in_=acc[:]), 16)
                g.wait_ge(LDo.h, LDo.n)

    return nc


def _host_inputs(L0_attn_x, L0_ffn_gate_w, L0_ffn_w1, L0_ffn_w2, L0_ffn_w3,
                 L0_ffn_ws1, L0_ffn_ws2, L0_ffn_ws3):
    xf = np.ascontiguousarray(np.asarray(L0_attn_x, dtype=np.float32))
    gw = np.ascontiguousarray(np.asarray(L0_ffn_gate_w, dtype=np.float32))
    w1 = np.ascontiguousarray(np.asarray(L0_ffn_w1, dtype=np.float32))
    w2 = np.ascontiguousarray(np.asarray(L0_ffn_w2, dtype=np.float32))
    w3 = np.ascontiguousarray(np.asarray(L0_ffn_w3, dtype=np.float32))
    ws1 = np.ascontiguousarray(np.asarray(L0_ffn_ws1, dtype=np.float32))
    ws2 = np.ascontiguousarray(np.asarray(L0_ffn_ws2, dtype=np.float32))
    ws3 = np.ascontiguousarray(np.asarray(L0_ffn_ws3, dtype=np.float32))
    gwT = np.ascontiguousarray(gw.T)
    w1t = np.ascontiguousarray(w1.transpose(0, 2, 1))
    w2t = np.ascontiguousarray(w2.transpose(0, 2, 1))
    w3t = np.ascontiguousarray(w3.transpose(0, 2, 1))
    ws1t = np.ascontiguousarray(ws1.T)
    ws2t = np.ascontiguousarray(ws2.T)
    ws3t = np.ascontiguousarray(ws3.T)
    in_maps = []
    for c in range(8):
        b, h = c // 2, c % 2
        mine = xf[b, h * 1024:(h + 1) * 1024]
        other = xf[b, (1 - h) * 1024:(2 - h) * 1024]
        xc = np.ascontiguousarray(np.concatenate([mine, other], axis=0))
        in_maps.append(dict(x=xc, gwT=gwT, w1t=w1t, w2t=w2t, w3t=w3t,
                            ws1t=ws1t, ws2t=ws2t, ws3t=ws3t))
    return in_maps


def kernel(L0_attn_x, L0_ffn_gate_w, L0_ffn_w1, L0_ffn_w2, L0_ffn_w3,
           L0_ffn_ws1, L0_ffn_ws2, L0_ffn_ws3):
    in_maps = _host_inputs(L0_attn_x, L0_ffn_gate_w, L0_ffn_w1, L0_ffn_w2,
                           L0_ffn_w3, L0_ffn_ws1, L0_ffn_ws2, L0_ffn_ws3)
    nc = build()
    res = run_bass_kernel_spmd(nc, in_maps, core_ids=list(range(8)))
    out = np.empty((4, 2048, 2048), dtype=np.float32)
    for c in range(8):
        b, h = c // 2, c % 2
        out[b, h * 1024:(h + 1) * 1024] = res.results[c]["out"]
    return out


# version 13
# speedup vs baseline: 1.8285x; 1.0301x over previous
import numpy as np
from contextlib import ExitStack

import concourse.bass as bass
import concourse.mybir as mybir
from concourse.bass_utils import run_bass_kernel_spmd
from concourse.alu_op_type import AluOpType as OP

F32 = mybir.dt.float32
BF16 = mybir.dt.bfloat16
U32 = mybir.dt.uint32
AF = mybir.ActivationFunctionType
AX = mybir.AxisListType

P = 128
D = 2048
SEQ = 2048
TB = 16
TBM = 8
DC = 16
H = 1024
UB = 8
NE = 8
NEXP = 9
EPS = 1e-5
INV_D = 1.0 / D

NATW = 2
W1WIN = 4
W2WIN = 4


class Sem:
    def __init__(self, h):
        self.h = h
        self.n = 0

    def inc(self, instr, k=1):
        instr.then_inc(self.h, k)
        self.n += k
        return self.n


def flat2(ap):
    return ap.rearrange("p a b -> p (a b)")


def build(want_taps=(), stage="full"):
    nc = bass.Bass("TRN2", target_bir_lowering=False, debug=False)

    x = nc.dram_tensor("x", [SEQ, D], F32, kind="ExternalInput").ap()
    gwT = nc.dram_tensor("gwT", [D, NE], F32, kind="ExternalInput").ap()
    w1t = nc.dram_tensor("w1t", [NE, D, H], F32, kind="ExternalInput").ap()
    w2t = nc.dram_tensor("w2t", [NE, H, D], F32, kind="ExternalInput").ap()
    w3t = nc.dram_tensor("w3t", [NE, D, H], F32, kind="ExternalInput").ap()
    ws1t = nc.dram_tensor("ws1t", [D, H], F32, kind="ExternalInput").ap()
    ws2t = nc.dram_tensor("ws2t", [H, D], F32, kind="ExternalInput").ap()
    ws3t = nc.dram_tensor("ws3t", [D, H], F32, kind="ExternalInput").ap()
    out = nc.dram_tensor("out", [TBM * P, D], F32, kind="ExternalOutput").ap()

    x_t = x.rearrange("(tb p) d -> p tb d", p=P)
    out_t = out.rearrange("(tb p) d -> p tb d", p=P)

    with ExitStack() as top:
        e = top.enter_context
        acc = e(nc.sbuf_tensor("acc", [P, TBM, D], F32))
        identt = e(nc.sbuf_tensor("identt", [P, P], F32))
        selt = e(nc.sbuf_tensor("selt", [NE, NE * P], BF16))
        iota8 = e(nc.sbuf_tensor("iota8", [P, NE], F32))
        gT = e(nc.sbuf_tensor("gT", [P, DC, NE], BF16))
        gws = e(nc.sbuf_tensor("gws", [P, DC, NE], F32))
        st_sum = e(nc.sbuf_tensor("st_sum", [P, TB], F32))
        st_ssq = e(nc.sbuf_tensor("st_ssq", [P, TB], F32))
        st_var = e(nc.sbuf_tensor("st_var", [P, TB], F32))
        st_rstd = e(nc.sbuf_tensor("st_rstd", [P, TB], F32))
        st_nmr = e(nc.sbuf_tensor("st_nmr", [P, TB], F32))
        st_tmp = e(nc.sbuf_tensor("st_tmp", [P, TB], F32))
        sqscr = e(nc.sbuf_tensor("sqscr", [P, D], F32))
        sm_mxn = e(nc.sbuf_tensor("sm_mxn", [P, 2], F32))
        sm_rs = e(nc.sbuf_tensor("sm_rs", [P, 2], F32))
        sm_lnr = e(nc.sbuf_tensor("sm_lnr", [P, 2], F32))
        sm_b2 = e(nc.sbuf_tensor("sm_b2", [P, 2], F32))

        pA = e(nc.psum_tensor("pA", [P, 512 * 4], F32))
        pB = e(nc.psum_tensor("pB", [P, 512 * 4], F32))
        pAR = [pA, pB]

        LDa = Sem(e(nc.semaphore("LDa")))
        LDx = Sem(e(nc.semaphore("LDx")))
        LDg = Sem(e(nc.semaphore("LDg")))
        LDw1 = [Sem(e(nc.semaphore(f"LDw1_{i}"))) for i in range(2)]
        LDw3 = [Sem(e(nc.semaphore(f"LDw3_{i}"))) for i in range(2)]
        LDw2 = [Sem(e(nc.semaphore(f"LDw2_{i}"))) for i in range(2)]
        LDo = Sem(e(nc.semaphore("LDo")))
        GP = Sem(e(nc.semaphore("GP")))
        TP = Sem(e(nc.semaphore("TP")))
        PE = Sem(e(nc.semaphore("PE")))
        VE = Sem(e(nc.semaphore("VE")))
        SE = Sem(e(nc.semaphore("SE")))

        block = e(nc.Block())

        def tap(name, ap, waits=()):
            if name not in want_taps:
                return
            t = nc.dram_tensor("tap_" + name, list(ap.shape), ap.dtype,
                               kind="ExternalOutput").ap()

            @block.gpsimd
            def _(g, ap=ap, t=t, waits=tuple(waits)):
                for s, v in waits:
                    g.wait_ge(s.h, v)
                LDo.inc(g.dma_start(out=t, in_=ap), 16)

        @block.gpsimd
        def _(g):
            GP.inc(g.iota(iota8[:], pattern=[[1, NE]], base=0, channel_multiplier=0,
                          allow_small_or_imprecise_dtypes=True), 1)
            GP.inc(g.iota(identt[:], pattern=[[0, P]], base=0, channel_multiplier=1,
                          allow_small_or_imprecise_dtypes=True), 1)
            GP.inc(g.iota(sqscr[:, 0:P], pattern=[[1, P]], base=0, channel_multiplier=0,
                          allow_small_or_imprecise_dtypes=True), 1)
            GP.inc(g.iota(selt[:], pattern=[[0, NE * P]], base=0, channel_multiplier=1,
                          allow_small_or_imprecise_dtypes=True), 1)
            GP.inc(g.iota(sqscr[0:NE, P:P + NE * P].rearrange("k (e j) -> k e j", j=P),
                          pattern=[[1, NE], [0, P]], base=0, channel_multiplier=0,
                          allow_small_or_imprecise_dtypes=True), 1)
            LDg.inc(g.dma_start(out=gws[:], in_=gwT.rearrange("(m p) e -> p m e", p=P)), 16)
            LDa.inc(g.dma_start(out=acc[:], in_=x_t[:, 0:TBM, :]), 16)
        gp_consts = GP.n
        ld_gws = (LDg, LDg.n)
        ld_acc = (LDa, LDa.n)

        @block.vector
        def _(v):
            v.wait_ge(GP.h, gp_consts)
            v.wait_ge(ld_gws[0].h, ld_gws[1])
            VE.inc(v.tensor_tensor(identt[:], identt[:], sqscr[:, 0:P], OP.is_equal))
            VE.inc(v.tensor_tensor(selt[:], selt[:], sqscr[0:NE, P:P + NE * P], OP.is_equal))
            VE.inc(v.tensor_copy(gT[:], gws[:]))
        ve_consts = VE.n

        st_m2 = e(nc.sbuf_tensor("st_m2", [P, TB], F32))

        def emit_ln_group(items, ld_waits=None, scalar_pre=None, vector_pre=None):
            cols = [c for c, _, _ in items]
            lo, hi = min(cols), max(cols) + 1
            csl = slice(lo, hi)

            @block.vector
            def _(v):
                if vector_pre is not None:
                    v.wait_ge(vector_pre[0].h, vector_pre[1])
                for i, (c, src_ap, _) in enumerate(items):
                    if ld_waits is not None and ld_waits[i] is not None:
                        v.wait_ge(ld_waits[i][0].h, ld_waits[i][1])
                    VE.inc(v.tensor_reduce(st_sum[:, c:c + 1], src_ap, AX.X, OP.add,
                                           negate=True))
            v_red = VE.n

            @block.scalar
            def _(s):
                if scalar_pre is not None:
                    s.wait_ge(scalar_pre[0].h, scalar_pre[1])
                for i, (c, src_ap, _) in enumerate(items):
                    if ld_waits is not None and ld_waits[i] is not None:
                        s.wait_ge(ld_waits[i][0].h, ld_waits[i][1])
                    SE.inc(s.activation(out=sqscr[:], in_=src_ap, func=AF.Square,
                                        accum_out=st_ssq[:, c:c + 1]))
            s_sq = SE.n

            @block.vector
            def _(v):
                v.wait_ge(VE.h, v_red)
                v.wait_ge(SE.h, s_sq)
                VE.inc(v.tensor_scalar_mul(st_sum[:, csl], st_sum[:, csl], INV_D))
                VE.inc(v.tensor_scalar_mul(st_tmp[:, csl], st_ssq[:, csl], INV_D))
            v_d1 = VE.n

            @block.vector
            def _(v):
                v.wait_ge(VE.h, v_d1)
                VE.inc(v.tensor_tensor(st_m2[:, csl], st_sum[:, csl], st_sum[:, csl],
                                       OP.mult))
            v_d2 = VE.n

            @block.vector
            def _(v):
                v.wait_ge(VE.h, v_d2)
                VE.inc(v.tensor_tensor(st_var[:, csl], st_tmp[:, csl], st_m2[:, csl],
                                       OP.subtract))
            v_d3 = VE.n

            @block.vector
            def _(v):
                v.wait_ge(VE.h, v_d3)
                VE.inc(v.tensor_scalar_add(st_var[:, csl], st_var[:, csl], EPS))
            v_d4 = VE.n

            @block.scalar
            def _(s):
                s.wait_ge(VE.h, v_d4)
                SE.inc(s.activation(out=st_tmp[:, csl], in_=st_var[:, csl], func=AF.Sqrt))
            s_std = SE.n

            @block.vector
            def _(v):
                v.wait_ge(SE.h, s_std)
                VE.inc(v.reciprocal(st_rstd[:, csl], st_tmp[:, csl]))
            v_rec = VE.n

            @block.vector
            def _(v):
                v.wait_ge(VE.h, v_rec)
                VE.inc(v.tensor_tensor(st_nmr[:, csl], st_sum[:, csl], st_rstd[:, csl],
                                       OP.mult))
            v_nmr = VE.n

            outs = []
            for c, src_ap, dst_ap in items:
                @block.scalar
                def _(s, c=c, src_ap=src_ap, dst_ap=dst_ap):
                    s.wait_ge(VE.h, v_nmr)
                    SE.inc(s.activation(out=dst_ap, in_=src_ap, func=AF.Identity,
                                        bias=st_nmr[:, c:c + 1],
                                        scale=st_rstd[:, c:c + 1]))
                outs.append(SE.n)
            return outs

        def emit_ln(src_ap, dst_ap, col, ld_wait=None, scalar_pre=None, vector_pre=None):
            return emit_ln_group([(col, src_ap, dst_ap)], ld_waits=[ld_wait],
                                 scalar_pre=scalar_pre, vector_pre=vector_pre)[0]

        arena_free = [(None, 0), (None, 0)]
        with ExitStack() as attn_scope:
            ea = attn_scope.enter_context
            Xn = ea(nc.sbuf_tensor("Xn", [P, TB, D], BF16))
            XnT = ea(nc.sbuf_tensor("XnT", [P, DC, TBM, P], BF16))
            xring = [ea(nc.sbuf_tensor(f"xring{i}", [P, D], F32)) for i in range(2)]
            Asb = [ea(nc.sbuf_tensor("Asb0", [P, D], BF16))]
            ATsl = [ea(nc.sbuf_tensor("ATsl0", [P, DC, 4, P], BF16))]

            s_norm = emit_ln_group(
                [(tb, acc[:, tb, :], Xn[:, tb, :]) for tb in range(TBM)],
                ld_waits=[ld_acc] * TBM,
                scalar_pre=(VE, ve_consts), vector_pre=None)
            ring_free = [(None, 0), (None, 0)]
            for tb in range(TBM, TB):
                slot = tb % 2
                @block.gpsimd
                def _(g, tb=tb, slot=slot, gate=tuple(ring_free[slot])):
                    sm, val = gate
                    if sm is not None:
                        g.wait_ge(sm.h, val)
                    LDx.inc(g.dma_start(out=xring[slot][:], in_=x_t[:, tb, :]), 16)
                sn = emit_ln(xring[slot][:], Xn[:, tb, :], tb, ld_wait=(LDx, LDx.n))
                s_norm.append(sn)
                ring_free[slot] = (SE, sn)
            s_ln1_all = SE.n
            tap("ln1", Xn[:], waits=[(SE, s_ln1_all)])

            if stage == "ln1":
                @block.gpsimd
                def _(g):
                    g.wait_ge(LDo.h, LDo.n)
                return nc

            for tb in range(TBM):
                @block.sync
                def _(sy, tb=tb):
                    sy.wait_ge(SE.h, s_norm[tb])
                    TP.inc(sy.dma_start(out=XnT[:, :, tb, :], in_=Xn[:, tb, :],
                                        transpose=True), 16)
                    sy.wait_ge(TP.h, TP.n)
            tp_xnt = TP.n
            tap("xnt", XnT[:], waits=[(TP, tp_xnt)])

            if stage == "xnt":
                @block.gpsimd
                def _(g):
                    g.wait_ge(LDo.h, LDo.n)
                return nc

            accw = {}
            asb_free = [(None, 0)]
            atsl_free = [(None, 0)]
            tp_at = {}
            for g_r in range(4):
                for ibl in range(4):
                    gi = 4 * g_r + ibl
                    par = gi % 2
                    ps = pAR[par]

                    @block.tensor
                    def _(t, gi=gi, ps=ps, gate=tuple(arena_free[par])):
                        if gi == 2:
                            t.wait_ge(SE.h, s_ln1_all)
                        sm, val = gate
                        if sm is not None:
                            t.wait_ge(sm.h, val)
                        for ct in range(TB):
                            if gi < 2:
                                t.wait_ge(SE.h, s_norm[ct])
                            for jr in range(4):
                                ins = t.matmul(ps[:, jr * 512:(jr + 1) * 512],
                                               Xn[:, ct, gi * P:(gi + 1) * P],
                                               Xn[:, ct, jr * 512:(jr + 1) * 512],
                                               start=(ct == 0), stop=(ct == TB - 1),
                                               skip_group_check=True)
                        PE.inc(ins)
                    pe_gram = PE.n

                    if stage == "gram0" and gi == 0:
                        @block.vector
                        def _(v, ps=ps, pe_gram=pe_gram):
                            v.wait_ge(PE.h, pe_gram)
                            VE.inc(v.tensor_copy(sqscr[:], ps[:]))
                        tap("G0", sqscr[:], waits=[(VE, VE.n)])

                        @block.gpsimd
                        def _(g):
                            g.wait_ge(LDo.h, LDo.n)
                        return nc

                    mxn = sm_mxn[:, par:par + 1]
                    rs = sm_rs[:, par:par + 1]
                    lnr = sm_lnr[:, par:par + 1]
                    b2 = sm_b2[:, par:par + 1]

                    @block.vector
                    def _(v, ps=ps, pe_gram=pe_gram, mxn=mxn):
                        v.wait_ge(PE.h, pe_gram)
                        VE.inc(v.tensor_reduce(mxn, ps[:], AX.X, OP.max, negate=True))
                    v_mxn = VE.n

                    @block.scalar
                    def _(s, gi=gi, ps=ps, v_mxn=v_mxn, mxn=mxn, rs=rs,
                          gate=tuple(asb_free[0])):
                        s.wait_ge(VE.h, v_mxn)
                        sm, val = gate
                        if sm is not None:
                            s.wait_ge(sm.h, val)
                        SE.inc(s.activation(out=Asb[0][:], in_=ps[:], func=AF.Exp,
                                            bias=mxn, accum_out=rs))
                    s_e1 = SE.n
                    arena_free[par] = (SE, s_e1)

                    @block.vector
                    def _(v, s_e1=s_e1, rs=rs, lnr=lnr):
                        v.wait_ge(SE.h, s_e1)
                        VE.inc(v.reciprocal(lnr, rs))
                    v_rr = VE.n

                    @block.vector
                    def _(v, v_rr=v_rr, lnr=lnr):
                        v.wait_ge(VE.h, v_rr)
                        VE.inc(v.tensor_scalar(Asb[0][:], Asb[0][:], lnr, None, OP.mult))
                    s_e2 = (VE, VE.n)

                    if stage == "gram1" and gi == 0:
                        tap("A0", Asb[0][:], waits=[(SE, s_e2)])

                        @block.gpsimd
                        def _(g):
                            g.wait_ge(LDo.h, LDo.n)
                        return nc

                    @block.sync
                    def _(sy, g_r=g_r, ibl=ibl, gi=gi, s_e2=s_e2,
                          gate=tuple(atsl_free[0])):
                        sy.wait_ge(s_e2[0].h, s_e2[1])
                        sm, val = gate
                        if sm is not None and ibl == 0:
                            sy.wait_ge(sm.h, val)
                        TP.inc(sy.dma_start(out=ATsl[0][:, :, ibl, :],
                                            in_=Asb[0][:], transpose=True), 16)
                        sy.wait_ge(TP.h, TP.n)
                    asb_free[0] = (TP, TP.n)

                    if stage == "at1" and gi == 0:
                        tap("AT0", ATsl[0][:], waits=[(TP, TP.n)])

                        @block.gpsimd
                        def _(g):
                            g.wait_ge(LDo.h, LDo.n)
                        return nc
                tp_at[g_r] = TP.n

                for tb in range(TBM):
                    par = tb % 2
                    slot = pAR[par][:, 0:512]

                    @block.tensor
                    def _(t, g_r=g_r, tb=tb, slot=slot, gate=tuple(arena_free[par])):
                        t.wait_ge(TP.h, tp_at[g_r])
                        if g_r == 0 and tb == 0:
                            t.wait_ge(TP.h, tp_xnt)
                        sm, val = gate
                        if sm is not None:
                            t.wait_ge(sm.h, val)
                        for m in range(DC):
                            ins = t.matmul(slot, XnT[:, m, tb, :],
                                           flat2(ATsl[0][:, m, :, :]),
                                           start=(m == 0), stop=(m == DC - 1))
                        PE.inc(ins)
                    pe_out = PE.n

                    @block.vector
                    def _(v, g_r=g_r, tb=tb, slot=slot, pe_out=pe_out):
                        v.wait_ge(PE.h, pe_out)
                        a_sl = acc[:, tb, g_r * 512:(g_r + 1) * 512]
                        VE.inc(v.tensor_tensor(a_sl, a_sl, slot, OP.add))
                    arena_free[par] = (VE, VE.n)
                    accw[(g_r, tb)] = VE.n
                atsl_free[0] = (PE, PE.n)
            ve_x1 = VE.n
            tap("x1", acc[:], waits=[(VE, ve_x1)])


        if stage == "attn":
            @block.gpsimd
            def _(g):
                g.wait_ge(LDo.h, LDo.n)
            return nc
        with ExitStack() as ffn_scope:
            ef = ffn_scope.enter_context
            xfT = ef(nc.sbuf_tensor("xfT", [P, DC, TBM, P], BF16))
            xfring = ef(nc.sbuf_tensor("xfring", [P, 2, D], BF16))
            scores = ef(nc.sbuf_tensor("scores", [P, TBM, NE], F32))
            topv = ef(nc.sbuf_tensor("topv", [P, TBM, NE], F32))
            topi = ef(nc.sbuf_tensor("topi", [P, TBM, NE], U32))
            topf = ef(nc.sbuf_tensor("topf", [P, TBM, NE], F32))
            Ssc = ef(nc.sbuf_tensor("Ssc", [P, TBM, NE], F32))
            STb = ef(nc.sbuf_tensor("STb", [NE, TBM * P], BF16))
            Sb = ef(nc.sbuf_tensor("Sb", [P, NE, H], BF16))
            W1T = ef(nc.sbuf_tensor("W1T", [P, DC, 2, 256], BF16))
            W3T = ef(nc.sbuf_tensor("W3T", [P, DC, 2, 256], BF16))
            W2T = ef(nc.sbuf_tensor("W2T", [P, UB, 2, 512], BF16))
            hbuf = ef(nc.sbuf_tensor("hbuf", [P, UB, H], BF16))
            tmp1 = ef(nc.sbuf_tensor("tmp1", [P, 512], F32))
            tmp3 = ef(nc.sbuf_tensor("tmp3", [P, 512], F32))
            silb = ef(nc.sbuf_tensor("silb", [P, 512], F32))

            tp_xf = []
            gate_prev = (None, 0)
            for grp in range(4):
                tbs = range(grp * 2, grp * 2 + 2)
                sns = emit_ln_group(
                    [(tb, acc[:, tb, :], xfring[:, tb % 2, :]) for tb in tbs],
                    scalar_pre=(VE, ve_x1) if grp == 0 else gate_prev,
                    vector_pre=(VE, ve_x1) if grp == 0 else None)
                for j, tb in enumerate(tbs):
                    @block.sync
                    def _(sy, tb=tb, sn=sns[j]):
                        sy.wait_ge(SE.h, sn)
                        TP.inc(sy.dma_start(out=xfT[:, :, tb, :],
                                            in_=xfring[:, tb % 2, :],
                                            transpose=True), 16)
                        sy.wait_ge(TP.h, TP.n)
                    tp_xf.append(TP.n)
                gate_prev = (TP, TP.n)
            tp_xfT = TP.n

            gate_arena0 = tuple(arena_free[0])
            gate_arena1 = tuple(arena_free[1])
            pe_gate = []
            s_gate_evac = []
            for tb in range(TBM):
                @block.tensor
                def _(t, tb=tb):
                    t.wait_ge(VE.h, ve_consts)
                    t.wait_ge(TP.h, tp_xf[tb])
                    if tb < 2:
                        sm, val = gate_arena0
                        if sm is not None:
                            t.wait_ge(sm.h, val)
                    if tb >= 2:
                        t.wait_ge(SE.h, s_gate_evac[tb - 2])
                    for m in range(DC):
                        ins = t.matmul(pAR[0][:, (tb % 2) * 512:(tb % 2) * 512 + NE],
                                       xfT[:, m, tb, :], gT[:, m, 0:NE],
                                       start=(m == 0), stop=(m == DC - 1))
                    PE.inc(ins)
                pe_gate.append(PE.n)

                @block.scalar
                def _(s, tb=tb):
                    s.wait_ge(PE.h, pe_gate[tb])
                    SE.inc(s.activation(out=scores[:, tb, :],
                                        in_=pAR[0][:, (tb % 2) * 512:(tb % 2) * 512 + NE],
                                        func=AF.Sigmoid))
                s_gate_evac.append(SE.n)
            s_scores = SE.n
            tap("scores", scores[:], waits=[(SE, s_scores)])

            if stage == "scores":
                @block.gpsimd
                def _(g):
                    g.wait_ge(LDo.h, LDo.n)
                return nc

            @block.vector
            def _(v):
                v.wait_ge(SE.h, s_scores)
                for tb in range(TBM):
                    VE.inc(v.max(topv[:, tb, :], scores[:, tb, :]))
            v_max = VE.n

            @block.vector
            def _(v):
                v.wait_ge(VE.h, v_max)
                for tb in range(TBM):
                    VE.inc(v.max_index(topi[:, tb, :], topv[:, tb, :], scores[:, tb, :]))
            v_mi = VE.n

            @block.vector
            def _(v):
                v.wait_ge(VE.h, v_mi)
                v.wait_ge(GP.h, 1)
                for tb in range(TBM):
                    VE.inc(v.tensor_copy(topf[:, tb, :], topi[:, tb, :]))
            v_tf = VE.n

            e3t = ef(nc.sbuf_tensor("e3t", [P, TBM, NE], F32))

            @block.vector
            def _(v):
                v.wait_ge(VE.h, v_tf)
                for tb in range(TBM):
                    VE.inc(v.tensor_scalar(Ssc[:, tb, :], iota8[:], topf[:, tb, 0:1],
                                           None, OP.is_equal))
                    VE.inc(v.tensor_scalar(e3t[:, tb, :], iota8[:], topf[:, tb, 1:2],
                                           None, OP.is_equal))
            v_eq = VE.n

            @block.vector
            def _(v):
                v.wait_ge(VE.h, v_eq)
                for tb in range(TBM):
                    VE.inc(v.tensor_scalar(Ssc[:, tb, :], Ssc[:, tb, :], topv[:, tb, 0:1],
                                           None, OP.mult))
            v_s1 = VE.n

            @block.vector
            def _(v):
                v.wait_ge(VE.h, v_s1)
                for tb in range(TBM):
                    VE.inc(v.scalar_tensor_tensor(Ssc[:, tb, :], e3t[:, tb, :],
                                                  topv[:, tb, 1:2], Ssc[:, tb, :],
                                                  OP.mult, OP.add))
            v_S = VE.n
            tap("S", Ssc[:], waits=[(VE, v_S)])

            ve_st = []
            for tb in range(TBM):
                @block.tensor
                def _(t, tb=tb):
                    t.wait_ge(VE.h, v_S)
                    t.wait_ge(VE.h, ve_consts)
                    if tb == 0:
                        sm, val = gate_arena1
                        if sm is not None:
                            t.wait_ge(sm.h, val)
                    else:
                        t.wait_ge(VE.h, ve_st[tb - 1])
                    PE.inc(t.transpose(pAR[1][0:NE, 0:P], Ssc[:, tb, :], identt[:]))
                pe_st = PE.n

                @block.vector
                def _(v, tb=tb, pe_st=pe_st):
                    v.wait_ge(PE.h, pe_st)
                    VE.inc(v.tensor_copy(STb[0:NE, tb * P:(tb + 1) * P],
                                         pAR[1][0:NE, 0:P]))
                ve_st.append(VE.n)
            v_stb = VE.n

            ve_sb_evac = []
            for ex in range(NE):
                @block.tensor
                def _(t, ex=ex):
                    t.wait_ge(VE.h, v_stb)
                    if ex >= 1:
                        t.wait_ge(VE.h, ve_sb_evac[ex - 1])
                    t.matmul(pAR[0][:, 0:512], selt[:, ex * P:(ex + 1) * P],
                             STb[0:NE, 0:512], start=True, stop=True)
                    ins = t.matmul(pAR[0][:, 512:1024], selt[:, ex * P:(ex + 1) * P],
                                   STb[0:NE, 512:1024], start=True, stop=True)
                    PE.inc(ins)
                pe_sb = PE.n

                @block.vector
                def _(v, ex=ex, pe_sb=pe_sb):
                    v.wait_ge(PE.h, pe_sb)
                    VE.inc(v.tensor_copy(Sb[:, ex, :], pAR[0][:, 0:1024]))
                ve_sb_evac.append(VE.n)
            v_sb_all = VE.n
            tap("Sb", Sb[:], waits=[(VE, v_sb_all)])

            if stage == "sb":
                @block.gpsimd
                def _(g):
                    g.wait_ge(LDo.h, LDo.n)
                return nc

            w1t_r = w1t.rearrange("E (m p) u -> E p m u", p=P)
            w3t_r = w3t.rearrange("E (m p) u -> E p m u", p=P)
            w2t_r = w2t.rearrange("E (mu p) d -> E p mu d", p=P)
            ws1t_r = ws1t.rearrange("(m p) u -> p m u", p=P)
            ws3t_r = ws3t.rearrange("(m p) u -> p m u", p=P)
            ws2t_r = ws2t.rearrange("(mu p) d -> p mu d", p=P)

            w1_free = [(VE, ve_x1), (VE, ve_x1)]
            w3_free = [(VE, ve_x1), (VE, ve_x1)]
            w2_free = [(VE, ve_x1), (VE, ve_x1)]
            w_ready = {}
            w_emitted = set()

            def load_w(ex, kind, idx):
                if ex >= NEXP or (ex, kind, idx) in w_emitted:
                    return
                w_emitted.add((ex, kind, idx))
                slot = idx % 2
                if kind == 1:
                    srcs = w1t_r[ex] if ex < NE else ws1t_r
                    srcap = srcs[:, :, idx * 256:(idx + 1) * 256]
                    dst = W1T[:, :, slot, :]
                    sem, gate = LDw1[slot], tuple(w1_free[slot])
                elif kind == 3:
                    srcs = w3t_r[ex] if ex < NE else ws3t_r
                    srcap = srcs[:, :, idx * 256:(idx + 1) * 256]
                    dst = W3T[:, :, slot, :]
                    sem, gate = LDw3[slot], tuple(w3_free[slot])
                else:
                    srcs = w2t_r[ex] if ex < NE else ws2t_r
                    srcap = srcs[:, :, idx * 512:(idx + 1) * 512]
                    dst = W2T[:, :, slot, :]
                    sem, gate = LDw2[slot], tuple(w2_free[slot])

                @block.gpsimd
                def _(g, srcap=srcap, dst=dst, sem=sem, gate=gate):
                    sm, val = gate
                    if sm is not None:
                        g.wait_ge(sm.h, val)
                    sem.inc(g.dma_start(out=dst, in_=srcap), 16)
                w_ready[(ex, kind, idx)] = (sem, sem.n)

            arena_free_z = [(VE, v_sb_all), (VE, v_stb)]
            last_sil = [0]
            last_hmul = [0]
            slotfree_r = [(None, 0), (None, 0)]
            h_readers_done = [0]

            for ex in range(NEXP):
                is_shared = (ex == NEXP - 1)
                for q in range(2):
                    load_w(ex, 1, q)
                    load_w(ex, 3, q)
                for kg in range(2):
                    load_w(ex, 2, kg)

                for ub in range(UB):
                    par = ub % 2
                    ps = pAR[par]

                    q = ub // 2
                    uo = (ub % 2) * P

                    @block.tensor
                    def _(t, ex=ex, ub=ub, q=q, uo=uo, ps=ps,
                          gate=tuple(arena_free_z[par])):
                        r1 = w_ready[(ex, 1, q)]
                        r3 = w_ready[(ex, 3, q)]
                        t.wait_ge(r1[0].h, r1[1])
                        t.wait_ge(r3[0].h, r3[1])
                        sm, val = gate
                        if sm is not None:
                            t.wait_ge(sm.h, val)
                        for m in range(DC):
                            l1 = W1T[:, m, q % 2, uo:uo + P]
                            l3 = W3T[:, m, q % 2, uo:uo + P]
                            t.matmul(ps[:, 0:512], l1, flat2(xfT[:, m, 0:4, :]),
                                     start=(m == 0), stop=(m == DC - 1),
                                     skip_group_check=True)
                            t.matmul(ps[:, 512:1024], l1, flat2(xfT[:, m, 4:8, :]),
                                     start=(m == 0), stop=(m == DC - 1),
                                     skip_group_check=True)
                            t.matmul(ps[:, 1024:1536], l3, flat2(xfT[:, m, 0:4, :]),
                                     start=(m == 0), stop=(m == DC - 1),
                                     skip_group_check=True)
                            ins = t.matmul(ps[:, 1536:2048], l3, flat2(xfT[:, m, 4:8, :]),
                                           start=(m == 0), stop=(m == DC - 1),
                                           skip_group_check=True)
                        PE.inc(ins)
                    pe_z = PE.n
                    if ub % 2 == 1:
                        w1_free[q % 2] = (PE, pe_z)
                        w3_free[q % 2] = (PE, pe_z)
                        if q + 2 <= 3:
                            load_w(ex, 1, q + 2)
                            load_w(ex, 3, q + 2)
                        else:
                            load_w(ex + 1, 1, q - 2)
                            load_w(ex + 1, 3, q - 2)

                    for th in range(2):
                        z1 = ps[:, th * 512:(th + 1) * 512]
                        z3 = ps[:, 1024 + th * 512:1024 + (th + 1) * 512]
                        h_sl = hbuf[:, ub, th * 512:(th + 1) * 512]
                        if not is_shared:
                            sbs = Sb[:, ex, th * 512:(th + 1) * 512]

                            @block.vector
                            def _(v, pe_z=pe_z, z1=z1, sbs=sbs, ls=last_sil[0]):
                                v.wait_ge(PE.h, pe_z)
                                if ex == 0 and ub == 0 and th == 0:
                                    v.wait_ge(VE.h, v_sb_all)
                                if ls:
                                    v.wait_ge(SE.h, ls)
                                VE.inc(v.tensor_tensor(tmp1[:], z1, sbs, OP.mult))
                            v_t1 = VE.n

                            @block.scalar
                            def _(s, v_t1=v_t1):
                                s.wait_ge(VE.h, v_t1)
                                SE.inc(s.activation(out=silb[:], in_=tmp1[:], func=AF.Silu))
                            s_sil = SE.n
                            last_sil[0] = s_sil

                            @block.vector
                            def _(v, z3=z3, sbs=sbs, h_sl=h_sl, s_sil=s_sil,
                                  hrd=h_readers_done[0], lh=last_hmul[0]):
                                if lh:
                                    v.wait_ge(VE.h, lh)
                                v_t3 = VE.inc(v.tensor_tensor(tmp3[:], z3, sbs, OP.mult))
                                v.wait_ge(SE.h, s_sil)
                                v.wait_ge(PE.h, hrd)
                                v.wait_ge(VE.h, v_t3)
                                VE.inc(v.tensor_tensor(h_sl, silb[:], tmp3[:], OP.mult))
                        else:
                            @block.scalar
                            def _(s, pe_z=pe_z, z1=z1, lh=last_hmul[0]):
                                s.wait_ge(PE.h, pe_z)
                                if lh:
                                    s.wait_ge(VE.h, lh)
                                SE.inc(s.activation(out=silb[:], in_=z1, func=AF.Silu))
                            s_sil = SE.n
                            last_sil[0] = s_sil

                            @block.vector
                            def _(v, z3=z3, h_sl=h_sl, s_sil=s_sil,
                                  hrd=h_readers_done[0]):
                                v.wait_ge(SE.h, s_sil)
                                v.wait_ge(PE.h, hrd)
                                VE.inc(v.tensor_tensor(h_sl, silb[:], z3, OP.mult))
                        last_hmul[0] = VE.n
                    arena_free_z[par] = (VE, VE.n)
                ve_h_all = VE.n

                for kg in range(4):
                    for tb in range(TBM):
                        par = tb % 2
                        slot = pAR[par][:, 0:512]

                        @block.tensor
                        def _(t, ex=ex, kg=kg, tb=tb, slot=slot,
                              gate=tuple(slotfree_r[par]),
                              zgate=tuple(arena_free_z[par]), ve_h_all=ve_h_all):
                            r2 = w_ready[(ex, 2, kg)]
                            t.wait_ge(r2[0].h, r2[1])
                            t.wait_ge(VE.h, ve_h_all)
                            sm, val = gate
                            if sm is not None:
                                t.wait_ge(sm.h, val)
                            if kg == 0 and tb < 2:
                                sm2, val2 = zgate
                                if sm2 is not None:
                                    t.wait_ge(sm2.h, val2)
                            for uc in range(UB):
                                ins = t.matmul(slot, hbuf[:, uc, tb * P:(tb + 1) * P],
                                               W2T[:, uc, kg % 2, :],
                                               start=(uc == 0), stop=(uc == UB - 1))
                            PE.inc(ins)
                        pe_r = PE.n
                        if tb == TBM - 1:
                            w2_free[kg % 2] = (PE, pe_r)
                            if kg + 2 <= 3:
                                load_w(ex, 2, kg + 2)
                            else:
                                load_w(ex + 1, 2, kg - 2)

                        @block.vector
                        def _(v, kg=kg, tb=tb, slot=slot, pe_r=pe_r,
                              aw=accw[(kg, tb)]):
                            v.wait_ge(PE.h, pe_r)
                            v.wait_ge(VE.h, aw)
                            a_sl = acc[:, tb, kg * 512:(kg + 1) * 512]
                            VE.inc(v.tensor_tensor(a_sl, a_sl, slot, OP.add))
                        slotfree_r[par] = (VE, VE.n)
                        accw[(kg, tb)] = VE.n
                h_readers_done[0] = PE.n
                arena_free_z = [(VE, VE.n), (VE, VE.n)]
            ve_final = VE.n

            @block.gpsimd
            def _(g):
                g.wait_ge(VE.h, ve_final)
                LDo.inc(g.dma_start(out=out_t[:], in_=acc[:]), 16)
                g.wait_ge(LDo.h, LDo.n)

    return nc


def _host_inputs(L0_attn_x, L0_ffn_gate_w, L0_ffn_w1, L0_ffn_w2, L0_ffn_w3,
                 L0_ffn_ws1, L0_ffn_ws2, L0_ffn_ws3):
    xf = np.ascontiguousarray(np.asarray(L0_attn_x, dtype=np.float32))
    gw = np.ascontiguousarray(np.asarray(L0_ffn_gate_w, dtype=np.float32))
    w1 = np.ascontiguousarray(np.asarray(L0_ffn_w1, dtype=np.float32))
    w2 = np.ascontiguousarray(np.asarray(L0_ffn_w2, dtype=np.float32))
    w3 = np.ascontiguousarray(np.asarray(L0_ffn_w3, dtype=np.float32))
    ws1 = np.ascontiguousarray(np.asarray(L0_ffn_ws1, dtype=np.float32))
    ws2 = np.ascontiguousarray(np.asarray(L0_ffn_ws2, dtype=np.float32))
    ws3 = np.ascontiguousarray(np.asarray(L0_ffn_ws3, dtype=np.float32))
    gwT = np.ascontiguousarray(gw.T)
    w1t = np.ascontiguousarray(w1.transpose(0, 2, 1))
    w2t = np.ascontiguousarray(w2.transpose(0, 2, 1))
    w3t = np.ascontiguousarray(w3.transpose(0, 2, 1))
    ws1t = np.ascontiguousarray(ws1.T)
    ws2t = np.ascontiguousarray(ws2.T)
    ws3t = np.ascontiguousarray(ws3.T)
    in_maps = []
    for c in range(8):
        b, h = c // 2, c % 2
        mine = xf[b, h * 1024:(h + 1) * 1024]
        other = xf[b, (1 - h) * 1024:(2 - h) * 1024]
        xc = np.ascontiguousarray(np.concatenate([mine, other], axis=0))
        in_maps.append(dict(x=xc, gwT=gwT, w1t=w1t, w2t=w2t, w3t=w3t,
                            ws1t=ws1t, ws2t=ws2t, ws3t=ws3t))
    return in_maps


def kernel(L0_attn_x, L0_ffn_gate_w, L0_ffn_w1, L0_ffn_w2, L0_ffn_w3,
           L0_ffn_ws1, L0_ffn_ws2, L0_ffn_ws3):
    in_maps = _host_inputs(L0_attn_x, L0_ffn_gate_w, L0_ffn_w1, L0_ffn_w2,
                           L0_ffn_w3, L0_ffn_ws1, L0_ffn_ws2, L0_ffn_ws3)
    nc = build()
    res = run_bass_kernel_spmd(nc, in_maps, core_ids=list(range(8)))
    out = np.empty((4, 2048, 2048), dtype=np.float32)
    for c in range(8):
        b, h = c // 2, c % 2
        out[b, h * 1024:(h + 1) * 1024] = res.results[c]["out"]
    return out
